# version 21
# baseline (speedup 1.0000x reference)
import contextlib
import numpy as np
import ml_dtypes
import concourse.bass as bass
import concourse.mybir as mybir
from concourse.bass_utils import run_bass_kernel_spmd

F32 = mybir.dt.float32
BF16 = mybir.dt.bfloat16
AF = mybir.ActivationFunctionType
OP = mybir.AluOpType
AX = mybir.AxisListType

P = 128
D = 2048
NCH = 16
TOK = 1024
HAL = 128
TT = TOK + HAL
NTILE = TT // P
NCORE = 8
MODW = 12288 // NCORE
INV_D = 1.0 / D


class Ctx:

    def __init__(self):
        self.marks = {}
        self.mode = "count"
        self.cnt = {}
        self.eng = {}
        self.sem = {}

    def start(self, mode):
        self.mode = mode
        self.cnt = {k: 0 for k in ("c", "x0", "x1", "x2", "a", "e", "g", "o", "w0", "w1", "wo0", "wo1", "cc", "t", "v", "s")}

    def op(self, fn, inc=None, mark=None, waits=(), eng=None):
        selfser = inc is not None and inc[0] in ("v", "s")
        if self.mode == "emit":
            e = None
            if selfser:
                e = self.eng["ve"] if inc[0] == "v" else self.eng["sc"]
            elif waits:
                e = self.eng[eng]
            for sk, mk in waits:
                e.wait_ge(self.sem[sk], self.marks[mk])
            if selfser and self.cnt[inc[0]] > 0:
                e.wait_ge(self.sem[inc[0]], self.cnt[inc[0]])
            inst = fn() if fn is not None else None
            if inc is not None and inst is not None:
                inst.then_inc(self.sem[inc[0]], inc[1])
        if inc is not None:
            self.cnt[inc[0]] += inc[1]
        if mark is not None:
            if self.mode == "count":
                self.marks[mark] = self.cnt[inc[0]]
            else:
                assert self.marks[mark] == self.cnt[inc[0]], (
                    mark, self.marks[mark], self.cnt[inc[0]])

    def mark_here(self, name, sem_key):
        if self.mode == "count":
            self.marks[name] = self.cnt[sem_key]
        else:
            assert self.marks[name] == self.cnt[sem_key], name


def build_program():
    nc = bass.Bass("TRN2", target_bir_lowering=False)

    xh_d = nc.dram_tensor("xh", [NTILE, P, D], F32, kind="ExternalInput").ap()
    winT_d = nc.dram_tensor("winT", [NCH, P, D], F32, kind="ExternalInput").ap()
    woutT_d = nc.dram_tensor("woutT", [NCH, P, D], F32, kind="ExternalInput").ap()
    wffnT_d = nc.dram_tensor("wffnT", [NCH, P, D], F32, kind="ExternalInput").ap()
    wmod_d = nc.dram_tensor("wmod_sl", [NCH, P, MODW], F32, kind="ExternalInput").ap()
    cols_d = nc.dram_tensor("veccols", [P, 66], F32, kind="ExternalInput").ap()
    identb_d = nc.dram_tensor("identb", [P, P], BF16, kind="ExternalInput").ap()
    identf_d = nc.dram_tensor("identf", [P, P], F32, kind="ExternalInput").ap()
    onesrow_d = nc.dram_tensor("onesrow", [1, P], BF16, kind="ExternalInput").ap()
    onescol_d = nc.dram_tensor("onescol", [P, 1], BF16, kind="ExternalInput").ap()
    out_d = nc.dram_tensor("out", [TOK // P, P, D], F32, kind="ExternalOutput").ap()

    cc_a = nc.dram_tensor("cc_a", [1, MODW], F32, kind="Internal")
    scr_ga = nc.dram_tensor("scr_ga", [1, D], BF16, kind="Internal")
    scr_gab = nc.dram_tensor("scr_gab", [1, D], BF16, kind="Internal")
    cc_b = nc.dram_tensor("cc_b", [NCORE, MODW], F32, kind="Internal")
    cc_b96 = cc_b.ap().rearrange("r (c k) -> (r c) k", k=P)

    C = Ctx()

    with contextlib.ExitStack() as ST:
        DTSZ = {F32: 4, BF16: 2}
        _cur = [(nc.sbuf_base + 31) & ~31]

        def at(name, shape, dt, cursor):
            off = (cursor[0] + 31) & ~31
            nbytes = shape[1] * DTSZ[dt]
            cursor[0] = off + nbytes
            assert cursor[0] <= nc.sbuf_top, (name, cursor[0], nc.sbuf_top)
            return nc.alloc_sbuf_tensor_at(name, shape, dt, offset=off)

        def sb(name, shape, dt):
            return at(name, shape, dt, _cur)

        def psmk(name, shape, dt):
            return ST.enter_context(nc.psum_tensor(name, shape, dt))

        st_states = sb("st_states", [P, NCH * TOK], BF16)
        st_wout = sb("st_wout", [P, NCH * D], BF16)
        st_cols = sb("st_cols", [P, 352], F32)
        st_colsbf = sb("st_colsbf", [P, 16], BF16)
        st_gabrow = sb("st_gabrow", [1, D], BF16)
        st_onesrow = sb("st_onesrow", [1, P], BF16)
        st_onescol = sb("st_onescol", [P, 1], BF16)
        st_identb = sb("st_identb", [P, P], BF16)
        st_identf = sb("st_identf", [P, P], F32)
        st_lnmu = sb("st_lnmu", [P, NTILE], F32)
        st_lnr = sb("st_lnr", [P, NTILE], F32)
        st_lnb = sb("st_lnb", [P, NTILE], F32)
        st_ss = sb("st_ss", [P, 8], F32)

        CL = {}
        _b = 0
        for nm, w in [("decay", 16), ("b1", 16), ("b2", 16), ("cvec", 16),
                      ("mask", 1), ("eps", 1),
                      ("d", 16), ("omd", 16), ("omdm", 16),
                      ("onepca", 16), ("sa", 16), ("opcam", 16), ("sam", 16),
                      ("onepcb", 16), ("sbv", 16),
                      ("ga", 16), ("gb", 16), ("gbb2", 16), ("gba", 16),
                      ("t1", 16), ("t2", 16), ("siluc", 16)]:
            CL[nm] = _b
            _b += w
        assert _b <= 352

        def col(nm, k=0, w=1):
            return st_cols[:, CL[nm] + k: CL[nm] + k + w]

        psA0 = psmk("psA0", [P, 512], F32)
        psA1 = psmk("psA1", [P, 512], F32)
        psA2 = psmk("psA2", [P, 512], F32)
        psB0 = psmk("psB0", [P, 512], F32)
        psB1 = psmk("psB1", [P, 512], F32)
        psB2 = psmk("psB2", [P, 512], F32)
        psM = psmk("psM", [P, 512], F32)
        psTr = psmk("psTr", [P, 1024], BF16)

        _curA = [_cur[0]]
        _curB = [_cur[0]]

        def sbA(name, shape, dt):
            return at(name, shape, dt, _curA)

        def sbB(name, shape, dt):
            return at(name, shape, dt, _curB)

        sA_m1in = sbA("sA_m1in", [P, NCH * TT], BF16)
        sA_xst = sbA("sA_xst", [P, 2 * D], F32)
        sA_wmst = sbA("sA_wmst", [P, 2 * MODW], F32)
        sA_wist = sbA("sA_wist", [P, 2 * D], BF16)
        sA_scanin = sbA("sA_scanin", [P, TOK], F32)
        sA_scanh = sbA("sA_scanh", [P, HAL], F32)
        sA_hscr = sbA("sA_hscr", [P, HAL], F32)
        sA_ccrow = sbA("sA_ccrow", [1, MODW], F32)
        sA_bga = sbA("sA_bga", [P, D], BF16)
        sA_ext = sbA("sA_ext", [16, 6 * P], F32)
        sA_rows16 = sbA("sA_rows16", [16, 2 * P], BF16)
        sA_garow = sbA("sA_garow", [1, D], BF16)
        sA_n1 = sbA("sA_n1", [P, 2 * D], BF16)

        sB_x1 = sbB("sB_x1", [P, 8 * D], F32)
        sB_xst = sbB("sB_xst", [P, D], F32)
        sB_wblk = sbB("sB_wblk", [P, 2 * D], BF16)
        sB_sgn = sbB("sB_sgn", [P, 2 * D], BF16)
        sB_ffnsc = sbB("sB_ffnsc", [P, 1024], BF16)
        sB_sq = sbB("sB_sq", [P, D], BF16)
        sB_n2 = sbB("sB_n2", [P, 2 * D], BF16)
        sB_arow = sbB("sB_arow", [1, P], F32)

        for key in ("c", "x0", "x1", "x2", "a", "e", "g", "o",
                    "w0", "w1", "wo0", "wo1", "cc", "t", "v", "s"):
            C.sem[key] = ST.enter_context(nc.semaphore("sem_" + key))

        o = C.op

        def prog_sync(sync):
            C.eng["sync"] = sync
            o(lambda: sync.dma_start(out=st_cols[:, 0:66], in_=cols_d[:, :]),
              inc=("c", 16))
            o(lambda: sync.dma_start(out=st_identb[:], in_=identb_d[:, :]), inc=("c", 16))
            o(lambda: sync.dma_start(out=st_identf[:], in_=identf_d[:, :]), inc=("c", 16))
            o(lambda: sync.dma_start(out=st_onesrow[:], in_=onesrow_d[:, :]), inc=("c", 16))
            o(lambda: sync.dma_start(out=st_onescol[:], in_=onescol_d[:, :]),
              inc=("c", 16), mark="consts")
            for t in range(NTILE):
                waits = [("s", f"n1_{t - 2}")] if t >= 2 else ()
                o(lambda t=t: sync.dma_start(
                    out=sA_xst[:, (t % 2) * D:(t % 2) * D + D], in_=xh_d[t]),
                  inc=("x" + str(t % 2), 16), mark=f"x_{t}", waits=waits, eng="sync")
            o(lambda: sync.dma_start(out=cc_a.ap()[:, :], in_=sA_ccrow[0:1, :]),
              inc=("a", 16), mark="ccA", waits=[("v", "ccrow")], eng="sync")
            src3 = cc_b.ap().rearrange("r c -> (r c)").rearrange(
                "(j i k) -> i j k", j=6, i=16, k=P)
            dst3 = sA_ext[0:16, 0:6 * P].rearrange("p (j k) -> p j k", j=6)
            o(lambda: sync.dma_start(out=dst3, in_=src3),
              inc=("e", 16), mark="ext_all",
              waits=[("cc", "cc_done")], eng="sync")
            o(lambda: sync.dma_start(
                out=scr_ga.ap().rearrange("a (i p) -> i p", i=16),
                in_=sA_rows16[0:16, 0:P]),
              inc=("g", 16), mark="garow_d", waits=[("v", "rows16")], eng="sync")
            o(lambda: sync.dma_start(out=st_gabrow[0:1, :], in_=sA_rows16[0:16, P:2 * P]),
              inc=("g", 16), mark="rows_all")
            o(lambda: sync.dma_start(
                out=sA_bga[:], in_=scr_ga.ap()[0:1, :].broadcast_to([P, D])),
              inc=("g", 16), mark="bga_dma", waits=[("g", "rows_all")], eng="sync")
            o(lambda: sync.dma_start(
                out=scr_gab.ap().rearrange("a (i p) -> i p", i=16),
                in_=sA_rows16[0:16, P:2 * P]),
              inc=("g", 16), mark="gabrow_d", waits=[("g", "bga_dma")],
              eng="sync")
            o(lambda: sync.dma_start(
                out=sA_bga[:], in_=scr_gab.ap()[0:1, :].broadcast_to([P, D])),
              inc=("g", 16), mark="bgab_dma", waits=[("v", "fold_all")],
              eng="sync")
            for t in range(8):
                waits = ([("v", "scan_all"), ("t", "mm1_all")] if t == 0
                         else [("v", f"ep1_{t - 1}")])
                o(lambda t=t: sync.dma_start(out=sB_xst[:], in_=xh_d[t + 1]),
                  inc=("x2", 16), mark=f"x2_{t}", waits=waits, eng="sync")
            out_r = out_d.rearrange("t p (m c) -> p m t c", c=P)
            x1_r = sB_x1[:, 0:8 * D].rearrange("p (t m c) -> p m t c", t=8, c=P)
            for m in range(NCH):
                o(lambda m=m: sync.dma_start(
                    out=out_r[:, m:m + 1], in_=x1_r[:, m:m + 1]),
                  inc=("o", 16), mark=f"out_{m}",
                  waits=[("v", f"addv_{m}")], eng="sync")
            if C.mode == "emit":
                sync.wait_ge(C.sem["o"], NCH * 16)

        def prog_gpsimd(gp):
            C.eng["gp"] = gp
            for c in range(NCH):
                waits = [("t", f"modmm_{c - 2}")] if c >= 2 else ()
                o(lambda c=c: gp.dma_start(
                    out=sA_wmst[:, (c % 2) * MODW:(c % 2) * MODW + MODW],
                    in_=wmod_d[c]),
                  inc=("w" + str(c % 2), 16), mark=f"wm_{c}", waits=waits, eng="gp")
            o(lambda: gp.collective_compute(
                "AllGather", OP.bypass,
                ins=[cc_a.ap()[:, :]], outs=[cc_b.ap()[:, :]],
                replica_groups=[list(range(NCORE))]),
              inc=("cc", 1), mark="cc_done", waits=[("a", "ccA")], eng="gp")
            for m in range(NCH):
                waits = [("t", f"mm1_{m - 2}")] if m >= 2 else ()
                o(lambda m=m: gp.dma_start(
                    out=sA_wist[:, (m % 2) * D:(m % 2) * D + D], in_=winT_d[m]),
                  inc=("w" + str(m % 2), 16), mark=f"wi_{m}", waits=waits, eng="gp")
                wo_waits = [("v", f"fold_{m - 2}")] if m >= 2 else ()
                o(lambda m=m: gp.dma_start(
                    out=st_wout[:, m * D:(m + 1) * D], in_=woutT_d[m]),
                  inc=("wo" + str(m % 2), 16), mark=f"wo_{m}", waits=wo_waits,
                  eng="gp")
            for m in range(NCH):
                if m <= 1:
                    waits = [("v", "scan_all")]
                else:
                    waits = [("t", f"alpha_{m - 2}")]
                o(lambda m=m: gp.dma_start(
                    out=sB_wblk[:, (m % 2) * D:(m % 2) * D + D], in_=wffnT_d[m]),
                  inc=("w" + str(m % 2), 16), mark=f"wf_{m}", waits=waits, eng="gp")

        def prog_scalar(sc):
            C.eng["sc"] = sc
            o(lambda: sc.activation(col("siluc", 0, 16), col("cvec", 0, 16), AF.Silu),
              inc=("s", 1), mark="siluc", waits=[("c", "consts")], eng="sc")
            o(lambda: sc.activation(col("d", 0, 16), col("decay", 0, 16), AF.Sigmoid),
              inc=("s", 1), mark="dsig")
            for t in range(NTILE):
                xs = sA_xst[:, (t % 2) * D:(t % 2) * D + D]
                n1 = sA_n1[:, (t % 2) * D:(t % 2) * D + D]
                waits = [("x" + str(t % 2), f"x_{t}")]
                if t >= 2:
                    waits.append(("t", f"tr1_{t - 2}"))
                o(lambda xs=xs, n1=n1: sc.activation(
                    n1, xs, AF.Square, accum_out=st_ss[:, 0:1]),
                  inc=("s", 1), mark=f"sq_{t}", waits=waits, eng="sc")
                o(lambda: sc.activation(st_ss[:, 3:4], st_ss[:, 1:2], AF.Sqrt,
                                        bias=col("eps")),
                  inc=("s", 1), mark=f"sd_{t}", waits=[("v", f"var_{t}")], eng="sc")
                o(lambda t=t, xs=xs, n1=n1: sc.activation(
                    n1, xs, AF.Identity,
                    bias=st_lnb[:, t:t + 1], scale=st_lnr[:, t:t + 1]),
                  inc=("s", 1), mark=f"n1_{t}", waits=[("v", f"rstd_{t}")], eng="sc")
            def s_trevac0(tt):
                for i in range(8):
                    waits = [("t", f"tr2r_{tt}_0")] if i == 0 else ()
                    o(lambda i=i, tt=tt: sc.activation(
                        st_states[:, i * TOK + tt * P:i * TOK + (tt + 1) * P],
                        psTr[:, i * P:(i + 1) * P], AF.Identity,
                        bias=col("sbv", i), scale=col("onepcb", i)),
                      inc=("s", 1),
                      mark=(f"ev2s_{tt}" if i == 7 else None),
                      waits=waits, eng="sc")

            for t in range(8):
                x1t = sB_x1[:, t * D:(t + 1) * D]
                if t > 1:
                    s_trevac0(t - 2)
                o(lambda x1t=x1t: sc.activation(
                    sB_sq[:], x1t, AF.Square, accum_out=st_ss[:, 0:1]),
                  inc=("s", 1), mark=f"sq2_{t}", waits=[("v", f"ep1_{t}")], eng="sc")
                o(lambda: sc.activation(st_ss[:, 3:4], st_ss[:, 2:3], AF.Sqrt,
                                        bias=col("eps")),
                  inc=("s", 1), mark=f"sd2_{t}", waits=[("v", f"var2_{t}")], eng="sc")
                waits = [("v", f"rstd2_{t}")]
                if t >= 2:
                    waits.append(("t", f"tr2_{t - 2}"))
                o(lambda t=t, x1t=x1t: sc.activation(
                    sB_n2[:, (t % 2) * D:(t % 2) * D + D], x1t, AF.Identity,
                    bias=st_ss[:, 5:6], scale=st_ss[:, 4:5]),
                  inc=("s", 1), mark=f"n2_{t}", waits=waits, eng="sc")
            s_trevac0(6)
            s_trevac0(7)
            def s_sgnabs(m):
                wb = sB_wblk[:, (m % 2) * D:(m % 2) * D + D]
                sg = sB_sgn[:, (m % 2) * D:(m % 2) * D + D]
                waits = [("w" + str(m % 2), f"wf_{m}")]
                if m >= 2:
                    waits.append(("t", f"mm3_{m - 2}"))
                o(lambda wb=wb, sg=sg: sc.sign(sg, wb),
                  inc=("s", 1), mark=f"sgn_{m}", waits=waits, eng="sc")
                o(lambda wb=wb: sc.activation(wb, wb, AF.Abs),
                  inc=("s", 1), mark=f"abs_{m}")

            s_sgnabs(0)
            s_sgnabs(1)
            for m in range(1, NCH):
                if m >= 2:
                    s_sgnabs(m)
                if m > 0:
                    mp = m - 1
                    fbp = [psA0, psA1] if mp % 2 == 0 else [psA2, psB0]
                    for j in range(2):
                        waits = ([("t", f"mm3_{mp}"), ("v", f"gba_{mp}")]
                                 if j == 0 else ())
                        o(lambda mp=mp, j=j, fbp=fbp: sc.activation(
                            sB_ffnsc[:, j * 512:(j + 1) * 512], fbp[j][:, :],
                            AF.Identity, bias=col("gbb2", mp),
                            scale=col("gba", mp)),
                          inc=("s", 1), mark=(f"fev_{mp}" if j == 1 else None),
                          waits=waits, eng="sc")
            mp = NCH - 1
            fbp = [psA2, psB0]
            for j in range(2):
                waits = ([("t", f"mm3_{mp}"), ("v", f"gba_{mp}")] if j == 0 else ())
                o(lambda mp=mp, j=j, fbp=fbp: sc.activation(
                    sB_ffnsc[:, j * 512:(j + 1) * 512], fbp[j][:, :],
                    AF.Identity, bias=col("gbb2", mp), scale=col("gba", mp)),
                  inc=("s", 1), mark=(f"fev_{mp}" if j == 1 else None),
                  waits=waits, eng="sc")

        def prog_tensor(te):
            C.eng["te"] = te

            def transpose_tile_n1(t):
                n1 = sA_n1[:, (t % 2) * D:(t % 2) * D + D]
                for r in range(2):
                    for i in range(8):
                        k = r * 8 + i
                        waits = []
                        if i == 0:
                            if r == 0:
                                waits.append(("s", f"n1_{t}"))
                            if t == 0 and r == 0:
                                waits.append(("c", "consts"))
                            if t > 0 or r > 0:
                                pt, pr = (t, 0) if r == 1 else (t - 1, 1)
                                waits.append(("v", f"ev1_{pt}_{pr}"))
                        o(lambda k=k, n1=n1: te.transpose(
                            psTr[:, (k % 8) * P:(k % 8) * P + P],
                            n1[:, k * P:(k + 1) * P], st_identb[:]),
                          inc=("t", 1),
                          mark=(f"tr1r_{t}_{r}" if i == 7 else None),
                          waits=waits, eng="te")
                C.mark_here(f"tr1_{t}", "t")

            def emit_mod_chunk(c):
                wst = sA_wmst[:, (c % 2) * MODW:(c % 2) * MODW + MODW]
                banks = [psB0, psB1, psB2]
                for j in range(3):
                    o(lambda c=c, j=j, wst=wst, bk=banks[j]: te.matmul(
                        bk[0:1, :], col("siluc", c),
                        wst[:, j * 512:(j + 1) * 512],
                        start=(c == 0), stop=(c == NCH - 1)),
                      inc=("t", 1), mark=(f"modmm_{c}" if j == 2 else None),
                      waits=([("w" + str(c % 2), f"wm_{c}"), ("s", "siluc")] if j == 0 else ()),
                      eng="te")

            order = [("mod", 0), ("mod", 1), ("mod", 2), ("mod", 3)]
            for t in range(NTILE):
                order.append(("tile", t))
                if 4 + 2 * t < NCH:
                    order.append(("mod", 4 + 2 * t))
                if 5 + 2 * t < NCH:
                    order.append(("mod", 5 + 2 * t))
            for kind, idx in order:
                if kind == "mod":
                    emit_mod_chunk(idx)
                else:
                    transpose_tile_n1(idx)

            for j in range(6):
                o(lambda j=j: te.transpose(
                    psM[:, j * 16:(j + 1) * 16],
                    sA_ext[0:16, j * P:(j + 1) * P], st_identf[0:16, 0:16]),
                  inc=("t", 1), mark=f"ccT_{j}",
                  waits=[("e", "ext_all")], eng="te")

            o(lambda: te.transpose(psM[0:16, 224:224 + P], col("ga", 0, 16),
                                   st_identf[:]),
              inc=("t", 1), mark="gaT",
              waits=[("v", "gacols"), ("v", "ccev")], eng="te")
            o(lambda: te.transpose(psM[0:16, 352:352 + P], col("t2", 0, 16),
                                   st_identf[:]),
              inc=("t", 1), mark="gabT", waits=[("v", "gabcols")], eng="te")

            for m in range(NCH):
                wst = sA_wist[:, (m % 2) * D:(m % 2) * D + D]
                pb = [psA0, psA1, psA2] if m % 2 == 0 else [psB0, psB1, psB2]
                segs = [(0, 512, 0), (512, 512, 1), (1024, HAL, 2)]
                for (off, ln, b) in segs:
                    for k in range(NCH):
                        waits = []
                        if k == 0 and off == 0:
                            waits = [("w" + str(m % 2), f"wi_{m}"), ("v", "affine_all")]
                            if m >= 2:
                                waits.append(("v", f"scanprep_{m - 2}"))
                            if m == 1:
                                waits.append(("v", "ccrow"))
                            if m == 0:
                                waits.append(("v", f"ev1_{NTILE - 1}_1"))
                        o(lambda k=k, off=off, ln=ln, b=b, wst=wst, pb=pb: te.matmul(
                            pb[b][:, 0:ln], wst[:, k * P:(k + 1) * P],
                            sA_m1in[:, k * TT + off:k * TT + off + ln],
                            start=(k == 0), stop=(k == NCH - 1)),
                          inc=("t", 1),
                          mark=(f"mm1_{m}" if (k == NCH - 1 and b == 2) else None),
                          waits=waits, eng="te")
            C.mark_here("mm1_all", "t")

            def transpose_round_n2(t, r):
                for i in range(8):
                    k = r * 8 + i
                    waits = []
                    if i == 0:
                        if r == 0:
                            waits.append(("s", f"n2_{t}"))
                            if t > 0:
                                waits.append(("v", f"ev2_{t - 1}_1"))
                            else:
                                waits.append(("v", f"ev1_{NTILE - 1}_1"))
                        else:
                            waits.append(("s", f"ev2s_{t}"))
                    o(lambda k=k, t=t: te.transpose(
                        psTr[:, (k % 8) * P:(k % 8) * P + P],
                        sB_n2[:, (t % 2) * D + k * P:(t % 2) * D + (k + 1) * P],
                        st_identb[:]),
                      inc=("t", 1),
                      mark=(f"tr2r_{t}_{r}" if i == 7 else None),
                      waits=waits, eng="te")
                if r == 1:
                    C.mark_here(f"tr2_{t}", "t")

            ybanks = [psA0, psA1, psA2, psB0]
            for t in range(8):
                for half in range(2):
                    for jj in range(2):
                        j = half * 2 + jj
                        for k in range(NCH):
                            waits = []
                            if k == 0 and jj == 0:
                                if t == 0 and half == 0:
                                    waits = [("v", "scan_all"), ("v", "fold_all"),
                                             ("v", "scanprep_15")]
                                elif t > 0:
                                    waits = [("v", f"ep1h_{t - 1}_{half}")]
                            o(lambda t=t, k=k, j=j: te.matmul(
                                ybanks[j][:, :],
                                st_states[:, k * TOK + t * P:k * TOK + (t + 1) * P],
                                st_wout[:, k * D + j * 512:k * D + (j + 1) * 512],
                                start=(k == 0), stop=(k == NCH - 1)),
                              inc=("t", 1),
                              mark=(f"mm2_{t}_{half}"
                                    if (jj == 1 and k == NCH - 1) else None),
                              waits=waits, eng="te")
                    if t > 1:
                        transpose_round_n2(t - 2, half)
            transpose_round_n2(6, 0)
            transpose_round_n2(6, 1)
            transpose_round_n2(7, 0)
            transpose_round_n2(7, 1)

            def emit_trf(mm):
                for i in range(8):
                    waits = []
                    if i == 0:
                        waits = [("s", f"fev_{mm}")]
                        waits.append(("v", f"addv_{mm - 1}") if mm > 0
                                     else ("v", "ev2_7_1"))
                    o(lambda mm=mm, i=i: te.transpose(
                        psTr[:, i * P:(i + 1) * P],
                        sB_ffnsc[:, i * P:(i + 1) * P], st_identb[:]),
                      inc=("t", 1), mark=(f"trf_{mm}" if i == 7 else None),
                      waits=waits, eng="te")

            for m in range(NCH):
                abank = psB1 if m % 2 == 0 else psB2
                fb = [psA0, psA1] if m % 2 == 0 else [psA2, psB0]
                for k in range(NCH):
                    waits = []
                    if k == 0:
                        waits = [("s", f"abs_{m}")]
                        if m >= 2:
                            waits.append(("v", f"aev1_{m - 2}"))
                    o(lambda m=m, k=k, ab=abank: te.matmul(
                        ab[0:1, 0:P], st_onescol[:, 0:1],
                        sB_wblk[:, (m % 2) * D + k * P:(m % 2) * D + (k + 1) * P],
                        start=(k == 0), stop=(k == NCH - 1)),
                      inc=("t", 1), mark=(f"alpha_{m}" if k == NCH - 1 else None),
                      waits=waits, eng="te")
                cnt = 0
                for j in range(2):
                    for k in range(NCH):
                        waits = []
                        if cnt == 0:
                            waits = [("s", f"sgn_{m}")]
                            if m == 0:
                                waits.append(("v", "ev2_3_1"))
                                waits.append(("v", "ep1h_7_0"))
                            if m >= 2:
                                waits.append(("s", f"fev_{m - 2}"))
                        if cnt == NCH and m == 0:
                            waits.append(("v", "tr2ev_all"))
                        o(lambda m=m, k=k, j=j, fb=fb: te.matmul(
                            fb[j][:, :],
                            sB_sgn[:, (m % 2) * D + k * P:(m % 2) * D + (k + 1) * P],
                            st_states[:, k * TOK + j * 512:k * TOK + (j + 1) * 512],
                            start=(k == 0), stop=(k == NCH - 1)),
                          inc=("t", 1),
                          mark=(f"mm3_{m}" if (k == NCH - 1 and j == 1) else None),
                          waits=waits, eng="te")
                        cnt += 1
                        if cnt == 8:
                            o(lambda m=m: te.transpose(
                                psM[:, 256 + (m % 2):256 + (m % 2) + 1],
                                sB_arow[0:1, :], st_identf[0:1, 0:1]),
                              inc=("t", 1), mark=f"alT_{m}",
                              waits=[("v", f"aev1_{m}")], eng="te")
                    if j == 0 and m > 0:
                        emit_trf(m - 1)
                    if j == 1 and m == NCH - 1:
                        emit_trf(NCH - 1)

        def prog_vector(ve):
            C.eng["ve"] = ve

            def ln_stats(xs_ap, mu_ap, r_ap, b_ap, e2_ap, mu2_ap,
                         varmark, rstdmark, sqm, sdm, waits0):
                o(lambda: ve.reduce_sum(mu_ap, xs_ap, axis=AX.X),
                  inc=("v", 1), waits=waits0, eng="ve")
                o(lambda: ve.tensor_scalar(mu_ap, mu_ap, INV_D, None, op0=OP.mult),
                  inc=("v", 1))
                o(lambda: ve.tensor_scalar(e2_ap, st_ss[:, 0:1], INV_D, None,
                                           op0=OP.mult),
                  inc=("v", 1), waits=[("s", sqm)], eng="ve")
                o(lambda: ve.tensor_mul(mu2_ap, mu_ap, mu_ap), inc=("v", 1))
                o(lambda: ve.tensor_sub(e2_ap, e2_ap, mu2_ap),
                  inc=("v", 1), mark=varmark)
                o(lambda: ve.reciprocal(r_ap, st_ss[:, 3:4]),
                  inc=("v", 1), waits=[("s", sdm)], eng="ve")
                o(lambda: ve.tensor_scalar(b_ap, mu_ap, r_ap, -1.0,
                                           op0=OP.mult, op1=OP.mult),
                  inc=("v", 1), mark=rstdmark)

            def evac_tile_n1(t):
                for r in range(2):
                    for i in range(8):
                        k = r * 8 + i
                        waits = [("t", f"tr1r_{t}_{r}")] if i == 0 else ()
                        o(lambda k=k, t=t: ve.tensor_copy(
                            sA_m1in[:, k * TT + t * P:k * TT + (t + 1) * P],
                            psTr[:, (k % 8) * P:(k % 8) * P + P]),
                          inc=("v", 1),
                          mark=(f"ev1_{t}_{r}" if i == 7 else None),
                          waits=waits, eng="ve")

            for t in range(NTILE):
                xs = sA_xst[:, (t % 2) * D:(t % 2) * D + D]
                ln_stats(xs, st_lnmu[:, t:t + 1], st_lnr[:, t:t + 1],
                         st_lnb[:, t:t + 1], st_ss[:, 1:2], st_ss[:, 2:3],
                         f"var_{t}", f"rstd_{t}", f"sq_{t}", f"sd_{t}",
                         [("x" + str(t % 2), f"x_{t}")])
                if t > 0:
                    evac_tile_n1(t - 1)
                if t == 6:
                    for j, bk in enumerate([psB0, psB1, psB2]):
                        o(lambda j=j, bk=bk: ve.tensor_copy(
                            sA_ccrow[0:1, j * 512:(j + 1) * 512], bk[0:1, :]),
                          inc=("v", 1), mark=("ccrow" if j == 2 else None),
                          waits=[("t", "modmm_15")] if j == 0 else (), eng="ve")
            evac_tile_n1(NTILE - 1)

            o(lambda: ve.tensor_copy(col("sa", 0, 16), psM[:, 0:16]),
              inc=("v", 1), waits=[("t", "ccT_5")], eng="ve")
            o(lambda: ve.tensor_scalar(col("onepca", 0, 16), psM[:, 16:32], 1.0,
                                       None, op0=OP.add), inc=("v", 1))
            def vclamp(psl, outnm, markn):
                o(lambda: ve.tensor_scalar(col("t1", 0, 16), psl, -1.0, None,
                                           op0=OP.mult), inc=("v", 1))
                o(lambda: ve.tensor_max(col("t1", 0, 16), col("t1", 0, 16), psl),
                  inc=("v", 1))
                o(lambda: ve.tensor_scalar(col("t2", 0, 16), col("t1", 0, 16),
                                           1e-30, None, op0=OP.max), inc=("v", 1))
                o(lambda: ve.reciprocal(col("t2", 0, 16), col("t2", 0, 16)),
                  inc=("v", 1))
                o(lambda: ve.tensor_mul(col("t2", 0, 16), col("t2", 0, 16), psl),
                  inc=("v", 1))
                o(lambda: ve.tensor_scalar(col("t1", 0, 16), col("t1", 0, 16),
                                           0.25, None, op0=OP.max), inc=("v", 1))
                o(lambda: ve.tensor_mul(col(outnm, 0, 16), col("t1", 0, 16),
                                        col("t2", 0, 16)),
                  inc=("v", 1), mark=markn)

            vclamp(psM[:, 32:48], "ga", "gacols")
            o(lambda: ve.tensor_copy(col("sbv", 0, 16), psM[:, 48:64]), inc=("v", 1))
            o(lambda: ve.tensor_scalar(col("onepcb", 0, 16), psM[:, 64:80], 1.0,
                                       None, op0=OP.add), inc=("v", 1))
            vclamp(psM[:, 80:96], "gb", "ccev")
            o(lambda: ve.tensor_mul(col("gbb2", 0, 16), col("gb", 0, 16),
                                    col("b2", 0, 16)), inc=("v", 1))
            o(lambda: ve.tensor_mul(col("t2", 0, 16), col("ga", 0, 16),
                                    col("b1", 0, 16)), inc=("v", 1),
              mark="gabcols")
            o(lambda: ve.tensor_scalar(col("omd", 0, 16), col("d", 0, 16), -1.0,
                                       1.0, op0=OP.mult, op1=OP.add),
              inc=("v", 1), waits=[("s", "dsig")], eng="ve")
            o(lambda: ve.tensor_scalar(col("omdm", 0, 16), col("omd", 0, 16),
                                       col("mask"), None, op0=OP.mult),
              inc=("v", 1))
            o(lambda: ve.tensor_scalar(col("opcam", 0, 16), col("onepca", 0, 16),
                                       col("mask"), None, op0=OP.mult), inc=("v", 1))
            o(lambda: ve.tensor_scalar(col("sam", 0, 16), col("sa", 0, 16),
                                       col("mask"), None, op0=OP.mult),
              inc=("v", 1), mark="affscal")

            o(lambda: ve.tensor_copy(sA_rows16[0:16, 0:P], psM[0:16, 224:224 + P]),
              inc=("v", 1), waits=[("t", "gabT")], eng="ve")
            o(lambda: ve.tensor_copy(sA_rows16[0:16, P:2 * P],
                                     psM[0:16, 352:352 + P]),
              inc=("v", 1), mark="rows16")

            for k in range(NCH):
                waits = []
                if k == 0:
                    waits = [("v", f"ev1_{NTILE - 1}_1"), ("v", "affscal")]
                o(lambda k=k: ve.tensor_scalar(
                    sA_m1in[:, k * TT:k * TT + HAL],
                    sA_m1in[:, k * TT:k * TT + HAL],
                    col("opcam", k), col("sam", k), op0=OP.mult, op1=OP.add),
                  inc=("v", 1), waits=waits, eng="ve")
                o(lambda k=k: ve.tensor_scalar(
                    sA_m1in[:, k * TT + HAL:(k + 1) * TT],
                    sA_m1in[:, k * TT + HAL:(k + 1) * TT],
                    col("onepca", k), col("sa", k), op0=OP.mult, op1=OP.add),
                  inc=("v", 1),
                  mark=("affine_all" if k == NCH - 1 else None))

            for m in range(NCH):
                pb = [psA0, psA1, psA2] if m % 2 == 0 else [psB0, psB1, psB2]
                waits = [("wo" + str(m % 2), f"wo_{m}")]
                if m == 0:
                    waits.append(("g", "bga_dma"))
                o(lambda m=m: ve.tensor_mul(
                    st_wout[:, m * D:(m + 1) * D], st_wout[:, m * D:(m + 1) * D],
                    sA_bga[:]),
                  inc=("v", 1), mark=(f"fold_{m}"), waits=waits, eng="ve")
                if m == NCH - 1:
                    C.mark_here("fold_all", "v")
                o(lambda m=m, pb=pb: ve.tensor_scalar(
                    sA_scanh[:], pb[0][:, 0:HAL], col("omdm", m), None,
                    op0=OP.mult),
                  inc=("v", 1), waits=[("t", f"mm1_{m}")], eng="ve")
                o(lambda m=m, pb=pb: ve.tensor_scalar(
                    sA_scanin[:, 0:384], pb[0][:, HAL:512], col("omd", m), None,
                    op0=OP.mult), inc=("v", 1))
                o(lambda m=m, pb=pb: ve.tensor_scalar(
                    sA_scanin[:, 384:896], pb[1][:, :], col("omd", m), None,
                    op0=OP.mult), inc=("v", 1))
                o(lambda m=m, pb=pb: ve.tensor_scalar(
                    sA_scanin[:, 896:1024], pb[2][:, 0:HAL], col("omd", m), None,
                    op0=OP.mult),
                  inc=("v", 1), mark=f"scanprep_{m}")
                o(lambda m=m: ve.tensor_tensor_scan(
                    sA_hscr[:], col("d", m).broadcast_to([P, HAL]), sA_scanh[:],
                    0.0, op0=OP.mult, op1=OP.add), inc=("v", 1))
                o(lambda m=m: ve.tensor_tensor_scan(
                    st_states[:, m * TOK:(m + 1) * TOK],
                    col("d", m).broadcast_to([P, TOK]), sA_scanin[:],
                    sA_hscr[:, HAL - 1:HAL], op0=OP.mult, op1=OP.add),
                  inc=("v", 1), mark=(f"scan_all" if m == NCH - 1 else None))

            def trevac_tile(tt):
                r = 1
                for i in range(8):
                    k = r * 8 + i
                    waits = [("t", f"tr2r_{tt}_{r}")] if i == 0 else ()
                    o(lambda k=k, tt=tt: ve.tensor_scalar(
                        st_states[:, k * TOK + tt * P:k * TOK + (tt + 1) * P],
                        psTr[:, (k % 8) * P:(k % 8) * P + P],
                        col("onepcb", k), col("sbv", k),
                        op0=OP.mult, op1=OP.add),
                      inc=("v", 1),
                      mark=(f"ev2_{tt}_{r}" if i == 7 else None),
                      waits=waits, eng="ve")

            for t in range(8):
                ybanks = [psA0, psA1, psA2, psB0]
                xg_waits = [("x2", f"x2_{t}")]
                if t == 0:
                    xg_waits.append(("g", "bgab_dma"))
                o(lambda t=t: ve.tensor_add(sB_xst[:], sB_xst[:], sA_bga[:]),
                  inc=("v", 1), waits=xg_waits, eng="ve")
                for half in range(2):
                    for jj in range(2):
                        j = half * 2 + jj
                        waits = []
                        if jj == 0:
                            waits = [("t", f"mm2_{t}_{half}")]
                        o(lambda t=t, j=j, yb=ybanks[j]: ve.tensor_add(
                            sB_x1[:, t * D + j * 512:t * D + (j + 1) * 512],
                            yb[:, :], sB_xst[:, j * 512:(j + 1) * 512]),
                          inc=("v", 1),
                          mark=(f"ep1h_{t}_{half}" if jj == 1 else None),
                          waits=waits, eng="ve")
                C.mark_here(f"ep1_{t}", "v")
                if t > 1:
                    trevac_tile(t - 2)
                x1t = sB_x1[:, t * D:(t + 1) * D]
                ln_stats(x1t, st_ss[:, 1:2], st_ss[:, 4:5], st_ss[:, 5:6],
                         st_ss[:, 2:3], st_ss[:, 3:4],
                         f"var2_{t}", f"rstd2_{t}", f"sq2_{t}", f"sd2_{t}", ())
            trevac_tile(6)
            trevac_tile(7)
            C.mark_here("tr2ev_all", "v")

            def emit_adds(mm):
                for t in range(8):
                    waits = [("t", f"trf_{mm}")] if t == 0 else ()
                    o(lambda mm=mm, t=t: ve.tensor_add(
                        sB_x1[:, t * D + mm * P:t * D + (mm + 1) * P],
                        sB_x1[:, t * D + mm * P:t * D + (mm + 1) * P],
                        psTr[:, t * P:(t + 1) * P]),
                      inc=("v", 1),
                      mark=(f"addv_{mm}" if t == 7 else None),
                      waits=waits, eng="ve")

            for m in range(NCH):
                abank = psB1 if m % 2 == 0 else psB2
                fb = [psA0, psA1] if m % 2 == 0 else [psA2, psB0]
                o(lambda ab=abank: ve.tensor_scalar(
                    sB_arow[0:1, :], ab[0:1, 0:P], INV_D, None, op0=OP.mult),
                  inc=("v", 1), mark=f"aev1_{m}",
                  waits=[("t", f"alpha_{m}")], eng="ve")
                o(lambda m=m: ve.tensor_scalar(
                    col("gba", m), psM[:, 256 + (m % 2):256 + (m % 2) + 1],
                    col("gb", m), None, op0=OP.mult),
                  inc=("v", 1), mark=f"gba_{m}",
                  waits=[("t", f"alT_{m}")], eng="ve")
                if m > 0:
                    emit_adds(m - 1)
            emit_adds(NCH - 1)

        progs = [prog_sync, prog_gpsimd, prog_scalar, prog_tensor, prog_vector]
        C.start("count")
        for f in progs:
            f(None)
        C.start("emit")
        block = ST.enter_context(nc.Block())
        block.sync(prog_sync)
        block.gpsimd(prog_gpsimd)
        block.scalar(prog_scalar)
        block.tensor(prog_tensor)
        block.vector(prog_vector)

    return nc


_NC_CACHE = []


def kernel(L0_attn_x, L0_attn_W_in, L0_attn_W_out, L0_attn_decay, L0_attn_bias,
           L0_ffn_weight, L0_ffn_bias, L0_c, L0_wmod):
    x = np.ascontiguousarray(np.asarray(L0_attn_x, dtype=np.float32).reshape(8192, D))
    W_in = np.asarray(L0_attn_W_in, dtype=np.float32)
    W_out = np.asarray(L0_attn_W_out, dtype=np.float32)
    decay = np.asarray(L0_attn_decay, dtype=np.float32)
    b1 = np.asarray(L0_attn_bias, dtype=np.float32)
    wffn = np.asarray(L0_ffn_weight, dtype=np.float32)
    b2 = np.asarray(L0_ffn_bias, dtype=np.float32)
    cvec = np.asarray(L0_c, dtype=np.float32).reshape(D)
    wmod = np.asarray(L0_wmod, dtype=np.float32)

    def blocks_swizzle(A):
        return np.ascontiguousarray(
            A.reshape(NCH, P, NCH, P).transpose(2, 1, 0, 3).reshape(NCH, P, D))

    winT = blocks_swizzle(W_in.T)
    wffnT = blocks_swizzle(wffn.T)
    woutT = np.ascontiguousarray(W_out.T.reshape(NCH, P, D))

    def cols_pack(v):
        return np.ascontiguousarray(v.reshape(NCH, P).T)

    identb = np.eye(P, dtype=ml_dtypes.bfloat16)
    identf = np.eye(P, dtype=np.float32)
    onesrow = np.ones((1, P), dtype=ml_dtypes.bfloat16)
    onescol = np.ones((P, 1), dtype=ml_dtypes.bfloat16)

    in_maps = []
    for c in range(NCORE):
        t0 = c * TOK
        halo = (np.zeros((HAL, D), np.float32) if c == 0
                else x[t0 - HAL:t0])
        xh = np.ascontiguousarray(
            np.concatenate([halo, x[t0:t0 + TOK]], axis=0).reshape(NTILE, P, D))
        wmod_sl = np.ascontiguousarray(
            wmod[:, c * MODW:(c + 1) * MODW].reshape(NCH, P, MODW))
        veccols = np.zeros((P, 66), np.float32)
        veccols[:, 0:16] = cols_pack(decay)
        veccols[:, 16:32] = cols_pack(b1)
        veccols[:, 32:48] = cols_pack(b2)
        veccols[:, 48:64] = cols_pack(cvec)
        veccols[:, 64] = 0.0 if c == 0 else 1.0
        veccols[:, 65] = 1e-5
        in_maps.append({
            "xh": xh, "winT": winT, "woutT": woutT, "wffnT": wffnT,
            "wmod_sl": wmod_sl, "veccols": veccols,
            "identb": identb, "identf": identf,
            "onesrow": onesrow, "onescol": onescol,
        })

    if not _NC_CACHE:
        _NC_CACHE.append(build_program())
    nc = _NC_CACHE[0]
    res = run_bass_kernel_spmd(nc, in_maps, core_ids=list(range(NCORE)))
    out = np.concatenate(
        [np.asarray(r["out"]).reshape(TOK, D) for r in res.results], axis=0)
    return out.reshape(1, 8192, D).astype(np.float32)


# version 22
# speedup vs baseline: 1.0017x; 1.0017x over previous
import contextlib
import numpy as np
import ml_dtypes
import concourse.bass as bass
import concourse.mybir as mybir
from concourse.bass_utils import run_bass_kernel_spmd

F32 = mybir.dt.float32
BF16 = mybir.dt.bfloat16
AF = mybir.ActivationFunctionType
OP = mybir.AluOpType
AX = mybir.AxisListType

P = 128
D = 2048
NCH = 16
TOK = 1024
HAL = 128
TT = TOK + HAL
NTILE = TT // P
NCORE = 8
MODW = 12288 // NCORE
INV_D = 1.0 / D


class Ctx:

    def __init__(self):
        self.marks = {}
        self.mode = "count"
        self.cnt = {}
        self.eng = {}
        self.sem = {}

    def start(self, mode):
        self.mode = mode
        self.cnt = {k: 0 for k in ("c", "x0", "x1", "x2", "a", "e", "g", "o", "w0", "w1", "wo0", "wo1", "cc", "t", "v", "s")}

    def op(self, fn, inc=None, mark=None, waits=(), eng=None):
        selfser = inc is not None and inc[0] in ("v", "s")
        if self.mode == "emit":
            e = None
            if selfser:
                e = self.eng["ve"] if inc[0] == "v" else self.eng["sc"]
            elif waits:
                e = self.eng[eng]
            for sk, mk in waits:
                e.wait_ge(self.sem[sk], self.marks[mk])
            if selfser and self.cnt[inc[0]] > 0:
                e.wait_ge(self.sem[inc[0]], self.cnt[inc[0]])
            inst = fn() if fn is not None else None
            if inc is not None and inst is not None:
                inst.then_inc(self.sem[inc[0]], inc[1])
        if inc is not None:
            self.cnt[inc[0]] += inc[1]
        if mark is not None:
            if self.mode == "count":
                self.marks[mark] = self.cnt[inc[0]]
            else:
                assert self.marks[mark] == self.cnt[inc[0]], (
                    mark, self.marks[mark], self.cnt[inc[0]])

    def mark_here(self, name, sem_key):
        if self.mode == "count":
            self.marks[name] = self.cnt[sem_key]
        else:
            assert self.marks[name] == self.cnt[sem_key], name


def build_program():
    nc = bass.Bass("TRN2", target_bir_lowering=False)

    xh_d = nc.dram_tensor("xh", [NTILE, P, D], F32, kind="ExternalInput").ap()
    winT_d = nc.dram_tensor("winT", [NCH, P, D], F32, kind="ExternalInput").ap()
    woutT_d = nc.dram_tensor("woutT", [NCH, P, D], F32, kind="ExternalInput").ap()
    wffnT_d = nc.dram_tensor("wffnT", [NCH, P, D], F32, kind="ExternalInput").ap()
    wmod_d = nc.dram_tensor("wmod_sl", [NCH, P, MODW], F32, kind="ExternalInput").ap()
    cols_d = nc.dram_tensor("veccols", [P, 66], F32, kind="ExternalInput").ap()
    identb_d = nc.dram_tensor("identb", [P, P], BF16, kind="ExternalInput").ap()
    identf_d = nc.dram_tensor("identf", [P, P], F32, kind="ExternalInput").ap()
    onesrow_d = nc.dram_tensor("onesrow", [1, P], BF16, kind="ExternalInput").ap()
    onescol_d = nc.dram_tensor("onescol", [P, 1], BF16, kind="ExternalInput").ap()
    out_d = nc.dram_tensor("out", [TOK // P, P, D], F32, kind="ExternalOutput").ap()

    cc_a = nc.dram_tensor("cc_a", [1, MODW], F32, kind="Internal")
    scr_ga = nc.dram_tensor("scr_ga", [1, D], BF16, kind="Internal")
    scr_gab = nc.dram_tensor("scr_gab", [1, D], BF16, kind="Internal")
    cc_b = nc.dram_tensor("cc_b", [NCORE, MODW], F32, kind="Internal")
    cc_b96 = cc_b.ap().rearrange("r (c k) -> (r c) k", k=P)

    C = Ctx()

    with contextlib.ExitStack() as ST:
        DTSZ = {F32: 4, BF16: 2}
        _cur = [(nc.sbuf_base + 31) & ~31]

        def at(name, shape, dt, cursor):
            off = (cursor[0] + 31) & ~31
            nbytes = shape[1] * DTSZ[dt]
            cursor[0] = off + nbytes
            assert cursor[0] <= nc.sbuf_top, (name, cursor[0], nc.sbuf_top)
            return nc.alloc_sbuf_tensor_at(name, shape, dt, offset=off)

        def sb(name, shape, dt):
            return at(name, shape, dt, _cur)

        def psmk(name, shape, dt):
            return ST.enter_context(nc.psum_tensor(name, shape, dt))

        st_states = sb("st_states", [P, NCH * TOK], BF16)
        st_wout = sb("st_wout", [P, NCH * D], BF16)
        st_cols = sb("st_cols", [P, 352], F32)
        st_colsbf = sb("st_colsbf", [P, 16], BF16)
        st_gabrow = sb("st_gabrow", [1, D], BF16)
        st_onesrow = sb("st_onesrow", [1, P], BF16)
        st_onescol = sb("st_onescol", [P, 1], BF16)
        st_identb = sb("st_identb", [P, P], BF16)
        st_identf = sb("st_identf", [P, P], F32)
        st_lnmu = sb("st_lnmu", [P, NTILE], F32)
        st_lnr = sb("st_lnr", [P, NTILE], F32)
        st_lnb = sb("st_lnb", [P, NTILE], F32)
        st_ss = sb("st_ss", [P, 8], F32)

        CL = {}
        _b = 0
        for nm, w in [("decay", 16), ("b1", 16), ("b2", 16), ("cvec", 16),
                      ("mask", 1), ("eps", 1),
                      ("d", 16), ("omd", 16), ("omdm", 16),
                      ("onepca", 16), ("sa", 16), ("opcam", 16), ("sam", 16),
                      ("onepcb", 16), ("sbv", 16),
                      ("ga", 16), ("gb", 16), ("gbb2", 16), ("gba", 16),
                      ("t1", 16), ("t2", 16), ("siluc", 16)]:
            CL[nm] = _b
            _b += w
        assert _b <= 352

        def col(nm, k=0, w=1):
            return st_cols[:, CL[nm] + k: CL[nm] + k + w]

        psA0 = psmk("psA0", [P, 512], F32)
        psA1 = psmk("psA1", [P, 512], F32)
        psA2 = psmk("psA2", [P, 512], F32)
        psB0 = psmk("psB0", [P, 512], F32)
        psB1 = psmk("psB1", [P, 512], F32)
        psB2 = psmk("psB2", [P, 512], F32)
        psM = psmk("psM", [P, 512], F32)
        psTr = psmk("psTr", [P, 1024], BF16)

        _curA = [_cur[0]]
        _curB = [_cur[0]]

        def sbA(name, shape, dt):
            return at(name, shape, dt, _curA)

        def sbB(name, shape, dt):
            return at(name, shape, dt, _curB)

        sA_m1in = sbA("sA_m1in", [P, NCH * TT], BF16)
        sA_xst = sbA("sA_xst", [P, 2 * D], F32)
        sA_wmst = sbA("sA_wmst", [P, 2 * MODW], F32)
        sA_wist = sbA("sA_wist", [P, 2 * D], BF16)
        sA_scanin = sbA("sA_scanin", [P, TOK], F32)
        sA_scanh = sbA("sA_scanh", [P, HAL], F32)
        sA_hscr = sbA("sA_hscr", [P, HAL], F32)
        sA_ccrow = sbA("sA_ccrow", [1, MODW], F32)
        sA_bga = sbA("sA_bga", [P, D], BF16)
        sA_ext = sbA("sA_ext", [16, 6 * P], F32)
        sA_rows16 = sbA("sA_rows16", [16, 2 * P], BF16)
        sA_garow = sbA("sA_garow", [1, D], BF16)
        sA_n1 = sbA("sA_n1", [P, 2 * D], BF16)

        sB_x1 = sbB("sB_x1", [P, 8 * D], F32)
        sB_xst = sbB("sB_xst", [P, D], F32)
        sB_wblk = sbB("sB_wblk", [P, 2 * D], BF16)
        sB_sgn = sbB("sB_sgn", [P, 2 * D], BF16)
        sB_ffnsc = sbB("sB_ffnsc", [P, 1024], BF16)
        sB_sq = sbB("sB_sq", [P, D], BF16)
        sB_n2 = sbB("sB_n2", [P, 2 * D], BF16)
        sB_arow = sbB("sB_arow", [1, P], F32)

        for key in ("c", "x0", "x1", "x2", "a", "e", "g", "o",
                    "w0", "w1", "wo0", "wo1", "cc", "t", "v", "s"):
            C.sem[key] = ST.enter_context(nc.semaphore("sem_" + key))

        o = C.op

        def prog_sync(sync):
            C.eng["sync"] = sync
            o(lambda: sync.dma_start(out=st_cols[:, 0:66], in_=cols_d[:, :]),
              inc=("c", 16))
            o(lambda: sync.dma_start(out=st_identb[:], in_=identb_d[:, :]), inc=("c", 16))
            o(lambda: sync.dma_start(out=st_identf[:], in_=identf_d[:, :]), inc=("c", 16))
            o(lambda: sync.dma_start(out=st_onesrow[:], in_=onesrow_d[:, :]), inc=("c", 16))
            o(lambda: sync.dma_start(out=st_onescol[:], in_=onescol_d[:, :]),
              inc=("c", 16), mark="consts")
            for t in range(NTILE):
                waits = [("s", f"n1_{t - 2}")] if t >= 2 else ()
                o(lambda t=t: sync.dma_start(
                    out=sA_xst[:, (t % 2) * D:(t % 2) * D + D], in_=xh_d[t]),
                  inc=("x" + str(t % 2), 16), mark=f"x_{t}", waits=waits, eng="sync")
            o(lambda: sync.dma_start(out=cc_a.ap()[:, :], in_=sA_ccrow[0:1, :]),
              inc=("a", 16), mark="ccA", waits=[("v", "ccrow")], eng="sync")
            src3 = cc_b.ap().rearrange("r c -> (r c)").rearrange(
                "(j i k) -> i j k", j=6, i=16, k=P)
            dst3 = sA_ext[0:16, 0:6 * P].rearrange("p (j k) -> p j k", j=6)
            o(lambda: sync.dma_start(out=dst3, in_=src3),
              inc=("e", 16), mark="ext_all",
              waits=[("cc", "cc_done")], eng="sync")
            o(lambda: sync.dma_start(
                out=scr_ga.ap().rearrange("a (i p) -> i p", i=16),
                in_=sA_rows16[0:16, 0:P]),
              inc=("g", 16), mark="garow_d", waits=[("v", "rows16")], eng="sync")
            o(lambda: sync.dma_start(out=st_gabrow[0:1, :], in_=sA_rows16[0:16, P:2 * P]),
              inc=("g", 16), mark="rows_all")
            o(lambda: sync.dma_start(
                out=sA_bga[:], in_=scr_ga.ap()[0:1, :].broadcast_to([P, D])),
              inc=("g", 16), mark="bga_dma", waits=[("g", "rows_all")], eng="sync")
            o(lambda: sync.dma_start(
                out=scr_gab.ap().rearrange("a (i p) -> i p", i=16),
                in_=sA_rows16[0:16, P:2 * P]),
              inc=("g", 16), mark="gabrow_d", waits=[("g", "bga_dma")],
              eng="sync")
            o(lambda: sync.dma_start(
                out=sA_bga[:], in_=scr_gab.ap()[0:1, :].broadcast_to([P, D])),
              inc=("g", 16), mark="bgab_dma",
              waits=[("v", "fold_all"), ("g", "gabrow_d")], eng="sync")
            for t in range(8):
                waits = ([("v", "scan_all"), ("t", "mm1_all")] if t == 0
                         else [("v", f"ep1_{t - 1}")])
                o(lambda t=t: sync.dma_start(out=sB_xst[:], in_=xh_d[t + 1]),
                  inc=("x2", 16), mark=f"x2_{t}", waits=waits, eng="sync")
            out_r = out_d.rearrange("t p (m c) -> p m t c", c=P)
            x1_r = sB_x1[:, 0:8 * D].rearrange("p (t m c) -> p m t c", t=8, c=P)
            for m in range(NCH):
                o(lambda m=m: sync.dma_start(
                    out=out_r[:, m:m + 1], in_=x1_r[:, m:m + 1]),
                  inc=("o", 16), mark=f"out_{m}",
                  waits=[("v", f"addv_{m}")], eng="sync")
            if C.mode == "emit":
                sync.wait_ge(C.sem["o"], NCH * 16)

        def prog_gpsimd(gp):
            C.eng["gp"] = gp
            for c in range(NCH):
                waits = [("t", f"modmm_{c - 2}")] if c >= 2 else ()
                o(lambda c=c: gp.dma_start(
                    out=sA_wmst[:, (c % 2) * MODW:(c % 2) * MODW + MODW],
                    in_=wmod_d[c]),
                  inc=("w" + str(c % 2), 16), mark=f"wm_{c}", waits=waits, eng="gp")
            o(lambda: gp.collective_compute(
                "AllGather", OP.bypass,
                ins=[cc_a.ap()[:, :]], outs=[cc_b.ap()[:, :]],
                replica_groups=[list(range(NCORE))]),
              inc=("cc", 1), mark="cc_done", waits=[("a", "ccA")], eng="gp")
            for m in range(NCH):
                waits = [("t", f"mm1_{m - 2}")] if m >= 2 else ()
                o(lambda m=m: gp.dma_start(
                    out=sA_wist[:, (m % 2) * D:(m % 2) * D + D], in_=winT_d[m]),
                  inc=("w" + str(m % 2), 16), mark=f"wi_{m}", waits=waits, eng="gp")
                wo_waits = [("v", f"fold_{m - 2}")] if m >= 2 else ()
                o(lambda m=m: gp.dma_start(
                    out=st_wout[:, m * D:(m + 1) * D], in_=woutT_d[m]),
                  inc=("wo" + str(m % 2), 16), mark=f"wo_{m}", waits=wo_waits,
                  eng="gp")
            for m in range(NCH):
                if m <= 1:
                    waits = [("v", "scan_all")]
                else:
                    waits = [("t", f"alpha_{m - 2}")]
                o(lambda m=m: gp.dma_start(
                    out=sB_wblk[:, (m % 2) * D:(m % 2) * D + D], in_=wffnT_d[m]),
                  inc=("w" + str(m % 2), 16), mark=f"wf_{m}", waits=waits, eng="gp")

        def prog_scalar(sc):
            C.eng["sc"] = sc
            o(lambda: sc.activation(col("siluc", 0, 16), col("cvec", 0, 16), AF.Silu),
              inc=("s", 1), mark="siluc", waits=[("c", "consts")], eng="sc")
            o(lambda: sc.activation(col("d", 0, 16), col("decay", 0, 16), AF.Sigmoid),
              inc=("s", 1), mark="dsig")
            for t in range(NTILE):
                xs = sA_xst[:, (t % 2) * D:(t % 2) * D + D]
                n1 = sA_n1[:, (t % 2) * D:(t % 2) * D + D]
                waits = [("x" + str(t % 2), f"x_{t}")]
                if t >= 2:
                    waits.append(("t", f"tr1_{t - 2}"))
                o(lambda xs=xs, n1=n1: sc.activation(
                    n1, xs, AF.Square, accum_out=st_ss[:, 0:1]),
                  inc=("s", 1), mark=f"sq_{t}", waits=waits, eng="sc")
                o(lambda: sc.activation(st_ss[:, 3:4], st_ss[:, 1:2], AF.Sqrt,
                                        bias=col("eps")),
                  inc=("s", 1), mark=f"sd_{t}", waits=[("v", f"var_{t}")], eng="sc")
                o(lambda t=t, xs=xs, n1=n1: sc.activation(
                    n1, xs, AF.Identity,
                    bias=st_lnb[:, t:t + 1], scale=st_lnr[:, t:t + 1]),
                  inc=("s", 1), mark=f"n1_{t}", waits=[("v", f"rstd_{t}")], eng="sc")
            def s_trevac0(tt):
                for i in range(8):
                    waits = [("t", f"tr2r_{tt}_0")] if i == 0 else ()
                    o(lambda i=i, tt=tt: sc.activation(
                        st_states[:, i * TOK + tt * P:i * TOK + (tt + 1) * P],
                        psTr[:, i * P:(i + 1) * P], AF.Identity,
                        bias=col("sbv", i), scale=col("onepcb", i)),
                      inc=("s", 1),
                      mark=(f"ev2s_{tt}" if i == 7 else None),
                      waits=waits, eng="sc")

            for t in range(8):
                x1t = sB_x1[:, t * D:(t + 1) * D]
                if t > 1:
                    s_trevac0(t - 2)
                o(lambda x1t=x1t: sc.activation(
                    sB_sq[:], x1t, AF.Square, accum_out=st_ss[:, 0:1]),
                  inc=("s", 1), mark=f"sq2_{t}", waits=[("v", f"ep1_{t}")], eng="sc")
                o(lambda: sc.activation(st_ss[:, 3:4], st_ss[:, 2:3], AF.Sqrt,
                                        bias=col("eps")),
                  inc=("s", 1), mark=f"sd2_{t}", waits=[("v", f"var2_{t}")], eng="sc")
                waits = [("v", f"rstd2_{t}")]
                if t >= 2:
                    waits.append(("t", f"tr2_{t - 2}"))
                o(lambda t=t, x1t=x1t: sc.activation(
                    sB_n2[:, (t % 2) * D:(t % 2) * D + D], x1t, AF.Identity,
                    bias=st_ss[:, 5:6], scale=st_ss[:, 4:5]),
                  inc=("s", 1), mark=f"n2_{t}", waits=waits, eng="sc")
            s_trevac0(6)
            s_trevac0(7)
            def s_sgnabs(m):
                wb = sB_wblk[:, (m % 2) * D:(m % 2) * D + D]
                sg = sB_sgn[:, (m % 2) * D:(m % 2) * D + D]
                waits = [("w" + str(m % 2), f"wf_{m}")]
                if m >= 2:
                    waits.append(("t", f"mm3_{m - 2}"))
                o(lambda wb=wb, sg=sg: sc.sign(sg, wb),
                  inc=("s", 1), mark=f"sgn_{m}", waits=waits, eng="sc")
                o(lambda wb=wb: sc.activation(wb, wb, AF.Abs),
                  inc=("s", 1), mark=f"abs_{m}")

            s_sgnabs(0)
            s_sgnabs(1)
            for m in range(1, NCH):
                if m >= 2:
                    s_sgnabs(m)
                if m > 0:
                    mp = m - 1
                    fbp = [psA0, psA1] if mp % 2 == 0 else [psA2, psB0]
                    for j in range(2):
                        waits = ([("t", f"mm3_{mp}"), ("v", f"gba_{mp}")]
                                 if j == 0 else ())
                        o(lambda mp=mp, j=j, fbp=fbp: sc.activation(
                            sB_ffnsc[:, j * 512:(j + 1) * 512], fbp[j][:, :],
                            AF.Identity, bias=col("gbb2", mp),
                            scale=col("gba", mp)),
                          inc=("s", 1), mark=(f"fev_{mp}" if j == 1 else None),
                          waits=waits, eng="sc")
            mp = NCH - 1
            fbp = [psA2, psB0]
            for j in range(2):
                waits = ([("t", f"mm3_{mp}"), ("v", f"gba_{mp}")] if j == 0 else ())
                o(lambda mp=mp, j=j, fbp=fbp: sc.activation(
                    sB_ffnsc[:, j * 512:(j + 1) * 512], fbp[j][:, :],
                    AF.Identity, bias=col("gbb2", mp), scale=col("gba", mp)),
                  inc=("s", 1), mark=(f"fev_{mp}" if j == 1 else None),
                  waits=waits, eng="sc")

        def prog_tensor(te):
            C.eng["te"] = te

            def transpose_tile_n1(t):
                n1 = sA_n1[:, (t % 2) * D:(t % 2) * D + D]
                for r in range(2):
                    for i in range(8):
                        k = r * 8 + i
                        waits = []
                        if i == 0:
                            if r == 0:
                                waits.append(("s", f"n1_{t}"))
                            if t == 0 and r == 0:
                                waits.append(("c", "consts"))
                            if t > 0 or r > 0:
                                pt, pr = (t, 0) if r == 1 else (t - 1, 1)
                                waits.append(("v", f"ev1_{pt}_{pr}"))
                        o(lambda k=k, n1=n1: te.transpose(
                            psTr[:, (k % 8) * P:(k % 8) * P + P],
                            n1[:, k * P:(k + 1) * P], st_identb[:]),
                          inc=("t", 1),
                          mark=(f"tr1r_{t}_{r}" if i == 7 else None),
                          waits=waits, eng="te")
                C.mark_here(f"tr1_{t}", "t")

            def emit_mod_chunk(c):
                wst = sA_wmst[:, (c % 2) * MODW:(c % 2) * MODW + MODW]
                banks = [psB0, psB1, psB2]
                for j in range(3):
                    o(lambda c=c, j=j, wst=wst, bk=banks[j]: te.matmul(
                        bk[0:1, :], col("siluc", c),
                        wst[:, j * 512:(j + 1) * 512],
                        start=(c == 0), stop=(c == NCH - 1)),
                      inc=("t", 1), mark=(f"modmm_{c}" if j == 2 else None),
                      waits=([("w" + str(c % 2), f"wm_{c}"), ("s", "siluc")] if j == 0 else ()),
                      eng="te")

            order = [("mod", 0), ("mod", 1), ("mod", 2), ("mod", 3)]
            for t in range(NTILE):
                order.append(("tile", t))
                if 4 + 2 * t < NCH:
                    order.append(("mod", 4 + 2 * t))
                if 5 + 2 * t < NCH:
                    order.append(("mod", 5 + 2 * t))
            for kind, idx in order:
                if kind == "mod":
                    emit_mod_chunk(idx)
                else:
                    transpose_tile_n1(idx)

            for j in range(6):
                o(lambda j=j: te.transpose(
                    psM[:, j * 16:(j + 1) * 16],
                    sA_ext[0:16, j * P:(j + 1) * P], st_identf[0:16, 0:16]),
                  inc=("t", 1), mark=f"ccT_{j}",
                  waits=[("e", "ext_all")], eng="te")

            o(lambda: te.transpose(psM[0:16, 224:224 + P], col("ga", 0, 16),
                                   st_identf[:]),
              inc=("t", 1), mark="gaT",
              waits=[("v", "gacols"), ("v", "ccev")], eng="te")
            o(lambda: te.transpose(psM[0:16, 352:352 + P], col("t2", 0, 16),
                                   st_identf[:]),
              inc=("t", 1), mark="gabT", waits=[("v", "gabcols")], eng="te")

            for m in range(NCH):
                wst = sA_wist[:, (m % 2) * D:(m % 2) * D + D]
                pb = [psA0, psA1, psA2] if m % 2 == 0 else [psB0, psB1, psB2]
                segs = [(0, 512, 0), (512, 512, 1), (1024, HAL, 2)]
                for (off, ln, b) in segs:
                    for k in range(NCH):
                        waits = []
                        if k == 0 and off == 0:
                            waits = [("w" + str(m % 2), f"wi_{m}"), ("v", "affine_all")]
                            if m >= 2:
                                waits.append(("v", f"scanprep_{m - 2}"))
                            if m == 1:
                                waits.append(("v", "ccrow"))
                            if m == 0:
                                waits.append(("v", f"ev1_{NTILE - 1}_1"))
                        o(lambda k=k, off=off, ln=ln, b=b, wst=wst, pb=pb: te.matmul(
                            pb[b][:, 0:ln], wst[:, k * P:(k + 1) * P],
                            sA_m1in[:, k * TT + off:k * TT + off + ln],
                            start=(k == 0), stop=(k == NCH - 1)),
                          inc=("t", 1),
                          mark=(f"mm1_{m}" if (k == NCH - 1 and b == 2) else None),
                          waits=waits, eng="te")
            C.mark_here("mm1_all", "t")

            def transpose_round_n2(t, r):
                for i in range(8):
                    k = r * 8 + i
                    waits = []
                    if i == 0:
                        if r == 0:
                            waits.append(("s", f"n2_{t}"))
                            if t > 0:
                                waits.append(("v", f"ev2_{t - 1}_1"))
                            else:
                                waits.append(("v", f"ev1_{NTILE - 1}_1"))
                        else:
                            waits.append(("s", f"ev2s_{t}"))
                    o(lambda k=k, t=t: te.transpose(
                        psTr[:, (k % 8) * P:(k % 8) * P + P],
                        sB_n2[:, (t % 2) * D + k * P:(t % 2) * D + (k + 1) * P],
                        st_identb[:]),
                      inc=("t", 1),
                      mark=(f"tr2r_{t}_{r}" if i == 7 else None),
                      waits=waits, eng="te")
                if r == 1:
                    C.mark_here(f"tr2_{t}", "t")

            ybanks = [psA0, psA1, psA2, psB0]
            for t in range(8):
                for half in range(2):
                    for jj in range(2):
                        j = half * 2 + jj
                        for k in range(NCH):
                            waits = []
                            if k == 0 and jj == 0:
                                if t == 0 and half == 0:
                                    waits = [("v", "scan_all"), ("v", "fold_all"),
                                             ("v", "scanprep_15")]
                                elif t > 0:
                                    waits = [("v", f"ep1h_{t - 1}_{half}")]
                            o(lambda t=t, k=k, j=j: te.matmul(
                                ybanks[j][:, :],
                                st_states[:, k * TOK + t * P:k * TOK + (t + 1) * P],
                                st_wout[:, k * D + j * 512:k * D + (j + 1) * 512],
                                start=(k == 0), stop=(k == NCH - 1)),
                              inc=("t", 1),
                              mark=(f"mm2_{t}_{half}"
                                    if (jj == 1 and k == NCH - 1) else None),
                              waits=waits, eng="te")
                    if t > 1:
                        transpose_round_n2(t - 2, half)
            transpose_round_n2(6, 0)
            transpose_round_n2(6, 1)
            transpose_round_n2(7, 0)
            transpose_round_n2(7, 1)

            def emit_trf(mm):
                for i in range(8):
                    waits = []
                    if i == 0:
                        waits = [("s", f"fev_{mm}")]
                        waits.append(("v", f"addv_{mm - 1}") if mm > 0
                                     else ("v", "ev2_7_1"))
                    o(lambda mm=mm, i=i: te.transpose(
                        psTr[:, i * P:(i + 1) * P],
                        sB_ffnsc[:, i * P:(i + 1) * P], st_identb[:]),
                      inc=("t", 1), mark=(f"trf_{mm}" if i == 7 else None),
                      waits=waits, eng="te")

            for m in range(NCH):
                abank = psB1 if m % 2 == 0 else psB2
                fb = [psA0, psA1] if m % 2 == 0 else [psA2, psB0]
                for k in range(NCH):
                    waits = []
                    if k == 0:
                        waits = [("s", f"abs_{m}")]
                        if m >= 2:
                            waits.append(("v", f"aev1_{m - 2}"))
                    o(lambda m=m, k=k, ab=abank: te.matmul(
                        ab[0:1, 0:P], st_onescol[:, 0:1],
                        sB_wblk[:, (m % 2) * D + k * P:(m % 2) * D + (k + 1) * P],
                        start=(k == 0), stop=(k == NCH - 1)),
                      inc=("t", 1), mark=(f"alpha_{m}" if k == NCH - 1 else None),
                      waits=waits, eng="te")
                cnt = 0
                for j in range(2):
                    for k in range(NCH):
                        waits = []
                        if cnt == 0:
                            waits = [("s", f"sgn_{m}")]
                            if m == 0:
                                waits.append(("v", "ev2_3_1"))
                                waits.append(("v", "ep1h_7_0"))
                            if m >= 2:
                                waits.append(("s", f"fev_{m - 2}"))
                        if cnt == NCH and m == 0:
                            waits.append(("v", "tr2ev_all"))
                        o(lambda m=m, k=k, j=j, fb=fb: te.matmul(
                            fb[j][:, :],
                            sB_sgn[:, (m % 2) * D + k * P:(m % 2) * D + (k + 1) * P],
                            st_states[:, k * TOK + j * 512:k * TOK + (j + 1) * 512],
                            start=(k == 0), stop=(k == NCH - 1)),
                          inc=("t", 1),
                          mark=(f"mm3_{m}" if (k == NCH - 1 and j == 1) else None),
                          waits=waits, eng="te")
                        cnt += 1
                        if cnt == 8:
                            o(lambda m=m: te.transpose(
                                psM[:, 256 + (m % 2):256 + (m % 2) + 1],
                                sB_arow[0:1, :], st_identf[0:1, 0:1]),
                              inc=("t", 1), mark=f"alT_{m}",
                              waits=[("v", f"aev1_{m}")], eng="te")
                    if j == 0 and m > 0:
                        emit_trf(m - 1)
                    if j == 1 and m == NCH - 1:
                        emit_trf(NCH - 1)

        def prog_vector(ve):
            C.eng["ve"] = ve

            def ln_stats(xs_ap, mu_ap, r_ap, b_ap, e2_ap, mu2_ap,
                         varmark, rstdmark, sqm, sdm, waits0):
                o(lambda: ve.reduce_sum(mu_ap, xs_ap, axis=AX.X),
                  inc=("v", 1), waits=waits0, eng="ve")
                o(lambda: ve.tensor_scalar(mu_ap, mu_ap, INV_D, None, op0=OP.mult),
                  inc=("v", 1))
                o(lambda: ve.tensor_scalar(e2_ap, st_ss[:, 0:1], INV_D, None,
                                           op0=OP.mult),
                  inc=("v", 1), waits=[("s", sqm)], eng="ve")
                o(lambda: ve.tensor_mul(mu2_ap, mu_ap, mu_ap), inc=("v", 1))
                o(lambda: ve.tensor_sub(e2_ap, e2_ap, mu2_ap),
                  inc=("v", 1), mark=varmark)
                o(lambda: ve.reciprocal(r_ap, st_ss[:, 3:4]),
                  inc=("v", 1), waits=[("s", sdm)], eng="ve")
                o(lambda: ve.tensor_scalar(b_ap, mu_ap, r_ap, -1.0,
                                           op0=OP.mult, op1=OP.mult),
                  inc=("v", 1), mark=rstdmark)

            def evac_tile_n1(t):
                for r in range(2):
                    for i in range(8):
                        k = r * 8 + i
                        waits = [("t", f"tr1r_{t}_{r}")] if i == 0 else ()
                        o(lambda k=k, t=t: ve.tensor_copy(
                            sA_m1in[:, k * TT + t * P:k * TT + (t + 1) * P],
                            psTr[:, (k % 8) * P:(k % 8) * P + P]),
                          inc=("v", 1),
                          mark=(f"ev1_{t}_{r}" if i == 7 else None),
                          waits=waits, eng="ve")

            for t in range(NTILE):
                xs = sA_xst[:, (t % 2) * D:(t % 2) * D + D]
                ln_stats(xs, st_lnmu[:, t:t + 1], st_lnr[:, t:t + 1],
                         st_lnb[:, t:t + 1], st_ss[:, 1:2], st_ss[:, 2:3],
                         f"var_{t}", f"rstd_{t}", f"sq_{t}", f"sd_{t}",
                         [("x" + str(t % 2), f"x_{t}")])
                if t > 0:
                    evac_tile_n1(t - 1)
                if t == 6:
                    for j, bk in enumerate([psB0, psB1, psB2]):
                        o(lambda j=j, bk=bk: ve.tensor_copy(
                            sA_ccrow[0:1, j * 512:(j + 1) * 512], bk[0:1, :]),
                          inc=("v", 1), mark=("ccrow" if j == 2 else None),
                          waits=[("t", "modmm_15")] if j == 0 else (), eng="ve")
            evac_tile_n1(NTILE - 1)

            o(lambda: ve.tensor_copy(col("sa", 0, 16), psM[:, 0:16]),
              inc=("v", 1), waits=[("t", "ccT_5")], eng="ve")
            o(lambda: ve.tensor_scalar(col("onepca", 0, 16), psM[:, 16:32], 1.0,
                                       None, op0=OP.add), inc=("v", 1))
            def vclamp(psl, outnm, markn):
                o(lambda: ve.tensor_scalar(col("t1", 0, 16), psl, -1.0, None,
                                           op0=OP.mult), inc=("v", 1))
                o(lambda: ve.tensor_max(col("t1", 0, 16), col("t1", 0, 16), psl),
                  inc=("v", 1))
                o(lambda: ve.tensor_scalar(col("t2", 0, 16), col("t1", 0, 16),
                                           1e-30, None, op0=OP.max), inc=("v", 1))
                o(lambda: ve.reciprocal(col("t2", 0, 16), col("t2", 0, 16)),
                  inc=("v", 1))
                o(lambda: ve.tensor_mul(col("t2", 0, 16), col("t2", 0, 16), psl),
                  inc=("v", 1))
                o(lambda: ve.tensor_scalar(col("t1", 0, 16), col("t1", 0, 16),
                                           0.25, None, op0=OP.max), inc=("v", 1))
                o(lambda: ve.tensor_mul(col(outnm, 0, 16), col("t1", 0, 16),
                                        col("t2", 0, 16)),
                  inc=("v", 1), mark=markn)

            vclamp(psM[:, 32:48], "ga", "gacols")
            o(lambda: ve.tensor_copy(col("sbv", 0, 16), psM[:, 48:64]), inc=("v", 1))
            o(lambda: ve.tensor_scalar(col("onepcb", 0, 16), psM[:, 64:80], 1.0,
                                       None, op0=OP.add), inc=("v", 1))
            vclamp(psM[:, 80:96], "gb", "ccev")
            o(lambda: ve.tensor_mul(col("gbb2", 0, 16), col("gb", 0, 16),
                                    col("b2", 0, 16)), inc=("v", 1))
            o(lambda: ve.tensor_mul(col("t2", 0, 16), col("ga", 0, 16),
                                    col("b1", 0, 16)), inc=("v", 1),
              mark="gabcols")
            o(lambda: ve.tensor_scalar(col("omd", 0, 16), col("d", 0, 16), -1.0,
                                       1.0, op0=OP.mult, op1=OP.add),
              inc=("v", 1), waits=[("s", "dsig")], eng="ve")
            o(lambda: ve.tensor_scalar(col("omdm", 0, 16), col("omd", 0, 16),
                                       col("mask"), None, op0=OP.mult),
              inc=("v", 1))
            o(lambda: ve.tensor_scalar(col("opcam", 0, 16), col("onepca", 0, 16),
                                       col("mask"), None, op0=OP.mult), inc=("v", 1))
            o(lambda: ve.tensor_scalar(col("sam", 0, 16), col("sa", 0, 16),
                                       col("mask"), None, op0=OP.mult),
              inc=("v", 1), mark="affscal")

            o(lambda: ve.tensor_copy(sA_rows16[0:16, 0:P], psM[0:16, 224:224 + P]),
              inc=("v", 1), waits=[("t", "gabT")], eng="ve")
            o(lambda: ve.tensor_copy(sA_rows16[0:16, P:2 * P],
                                     psM[0:16, 352:352 + P]),
              inc=("v", 1), mark="rows16")

            for k in range(NCH):
                waits = []
                if k == 0:
                    waits = [("v", f"ev1_{NTILE - 1}_1"), ("v", "affscal")]
                o(lambda k=k: ve.tensor_scalar(
                    sA_m1in[:, k * TT:k * TT + HAL],
                    sA_m1in[:, k * TT:k * TT + HAL],
                    col("opcam", k), col("sam", k), op0=OP.mult, op1=OP.add),
                  inc=("v", 1), waits=waits, eng="ve")
                o(lambda k=k: ve.tensor_scalar(
                    sA_m1in[:, k * TT + HAL:(k + 1) * TT],
                    sA_m1in[:, k * TT + HAL:(k + 1) * TT],
                    col("onepca", k), col("sa", k), op0=OP.mult, op1=OP.add),
                  inc=("v", 1),
                  mark=("affine_all" if k == NCH - 1 else None))

            for m in range(NCH):
                pb = [psA0, psA1, psA2] if m % 2 == 0 else [psB0, psB1, psB2]
                waits = [("wo" + str(m % 2), f"wo_{m}")]
                if m == 0:
                    waits.append(("g", "bga_dma"))
                o(lambda m=m: ve.tensor_mul(
                    st_wout[:, m * D:(m + 1) * D], st_wout[:, m * D:(m + 1) * D],
                    sA_bga[:]),
                  inc=("v", 1), mark=(f"fold_{m}"), waits=waits, eng="ve")
                if m == NCH - 1:
                    C.mark_here("fold_all", "v")
                o(lambda m=m, pb=pb: ve.tensor_scalar(
                    sA_scanh[:], pb[0][:, 0:HAL], col("omdm", m), None,
                    op0=OP.mult),
                  inc=("v", 1), waits=[("t", f"mm1_{m}")], eng="ve")
                o(lambda m=m, pb=pb: ve.tensor_scalar(
                    sA_scanin[:, 0:384], pb[0][:, HAL:512], col("omd", m), None,
                    op0=OP.mult), inc=("v", 1))
                o(lambda m=m, pb=pb: ve.tensor_scalar(
                    sA_scanin[:, 384:896], pb[1][:, :], col("omd", m), None,
                    op0=OP.mult), inc=("v", 1))
                o(lambda m=m, pb=pb: ve.tensor_scalar(
                    sA_scanin[:, 896:1024], pb[2][:, 0:HAL], col("omd", m), None,
                    op0=OP.mult),
                  inc=("v", 1), mark=f"scanprep_{m}")
                o(lambda m=m: ve.tensor_tensor_scan(
                    sA_hscr[:], col("d", m).broadcast_to([P, HAL]), sA_scanh[:],
                    0.0, op0=OP.mult, op1=OP.add), inc=("v", 1))
                o(lambda m=m: ve.tensor_tensor_scan(
                    st_states[:, m * TOK:(m + 1) * TOK],
                    col("d", m).broadcast_to([P, TOK]), sA_scanin[:],
                    sA_hscr[:, HAL - 1:HAL], op0=OP.mult, op1=OP.add),
                  inc=("v", 1), mark=(f"scan_all" if m == NCH - 1 else None))

            def trevac_tile(tt):
                r = 1
                for i in range(8):
                    k = r * 8 + i
                    waits = [("t", f"tr2r_{tt}_{r}")] if i == 0 else ()
                    o(lambda k=k, tt=tt: ve.tensor_scalar(
                        st_states[:, k * TOK + tt * P:k * TOK + (tt + 1) * P],
                        psTr[:, (k % 8) * P:(k % 8) * P + P],
                        col("onepcb", k), col("sbv", k),
                        op0=OP.mult, op1=OP.add),
                      inc=("v", 1),
                      mark=(f"ev2_{tt}_{r}" if i == 7 else None),
                      waits=waits, eng="ve")

            for t in range(8):
                ybanks = [psA0, psA1, psA2, psB0]
                xg_waits = [("x2", f"x2_{t}")]
                if t == 0:
                    xg_waits.append(("g", "bgab_dma"))
                o(lambda t=t: ve.tensor_add(sB_xst[:], sB_xst[:], sA_bga[:]),
                  inc=("v", 1), waits=xg_waits, eng="ve")
                for half in range(2):
                    for jj in range(2):
                        j = half * 2 + jj
                        waits = []
                        if jj == 0:
                            waits = [("t", f"mm2_{t}_{half}")]
                        o(lambda t=t, j=j, yb=ybanks[j]: ve.tensor_add(
                            sB_x1[:, t * D + j * 512:t * D + (j + 1) * 512],
                            yb[:, :], sB_xst[:, j * 512:(j + 1) * 512]),
                          inc=("v", 1),
                          mark=(f"ep1h_{t}_{half}" if jj == 1 else None),
                          waits=waits, eng="ve")
                C.mark_here(f"ep1_{t}", "v")
                if t > 1:
                    trevac_tile(t - 2)
                x1t = sB_x1[:, t * D:(t + 1) * D]
                ln_stats(x1t, st_ss[:, 1:2], st_ss[:, 4:5], st_ss[:, 5:6],
                         st_ss[:, 2:3], st_ss[:, 3:4],
                         f"var2_{t}", f"rstd2_{t}", f"sq2_{t}", f"sd2_{t}", ())
            trevac_tile(6)
            trevac_tile(7)
            C.mark_here("tr2ev_all", "v")

            def emit_adds(mm):
                for t in range(8):
                    waits = [("t", f"trf_{mm}")] if t == 0 else ()
                    o(lambda mm=mm, t=t: ve.tensor_add(
                        sB_x1[:, t * D + mm * P:t * D + (mm + 1) * P],
                        sB_x1[:, t * D + mm * P:t * D + (mm + 1) * P],
                        psTr[:, t * P:(t + 1) * P]),
                      inc=("v", 1),
                      mark=(f"addv_{mm}" if t == 7 else None),
                      waits=waits, eng="ve")

            for m in range(NCH):
                abank = psB1 if m % 2 == 0 else psB2
                fb = [psA0, psA1] if m % 2 == 0 else [psA2, psB0]
                o(lambda ab=abank: ve.tensor_scalar(
                    sB_arow[0:1, :], ab[0:1, 0:P], INV_D, None, op0=OP.mult),
                  inc=("v", 1), mark=f"aev1_{m}",
                  waits=[("t", f"alpha_{m}")], eng="ve")
                o(lambda m=m: ve.tensor_scalar(
                    col("gba", m), psM[:, 256 + (m % 2):256 + (m % 2) + 1],
                    col("gb", m), None, op0=OP.mult),
                  inc=("v", 1), mark=f"gba_{m}",
                  waits=[("t", f"alT_{m}")], eng="ve")
                if m > 0:
                    emit_adds(m - 1)
            emit_adds(NCH - 1)

        progs = [prog_sync, prog_gpsimd, prog_scalar, prog_tensor, prog_vector]
        C.start("count")
        for f in progs:
            f(None)
        C.start("emit")
        block = ST.enter_context(nc.Block())
        block.sync(prog_sync)
        block.gpsimd(prog_gpsimd)
        block.scalar(prog_scalar)
        block.tensor(prog_tensor)
        block.vector(prog_vector)

    return nc


_NC_CACHE = []


def kernel(L0_attn_x, L0_attn_W_in, L0_attn_W_out, L0_attn_decay, L0_attn_bias,
           L0_ffn_weight, L0_ffn_bias, L0_c, L0_wmod):
    x = np.ascontiguousarray(np.asarray(L0_attn_x, dtype=np.float32).reshape(8192, D))
    W_in = np.asarray(L0_attn_W_in, dtype=np.float32)
    W_out = np.asarray(L0_attn_W_out, dtype=np.float32)
    decay = np.asarray(L0_attn_decay, dtype=np.float32)
    b1 = np.asarray(L0_attn_bias, dtype=np.float32)
    wffn = np.asarray(L0_ffn_weight, dtype=np.float32)
    b2 = np.asarray(L0_ffn_bias, dtype=np.float32)
    cvec = np.asarray(L0_c, dtype=np.float32).reshape(D)
    wmod = np.asarray(L0_wmod, dtype=np.float32)

    def blocks_swizzle(A):
        return np.ascontiguousarray(
            A.reshape(NCH, P, NCH, P).transpose(2, 1, 0, 3).reshape(NCH, P, D))

    winT = blocks_swizzle(W_in.T)
    wffnT = blocks_swizzle(wffn.T)
    woutT = np.ascontiguousarray(W_out.T.reshape(NCH, P, D))

    def cols_pack(v):
        return np.ascontiguousarray(v.reshape(NCH, P).T)

    identb = np.eye(P, dtype=ml_dtypes.bfloat16)
    identf = np.eye(P, dtype=np.float32)
    onesrow = np.ones((1, P), dtype=ml_dtypes.bfloat16)
    onescol = np.ones((P, 1), dtype=ml_dtypes.bfloat16)

    in_maps = []
    for c in range(NCORE):
        t0 = c * TOK
        halo = (np.zeros((HAL, D), np.float32) if c == 0
                else x[t0 - HAL:t0])
        xh = np.ascontiguousarray(
            np.concatenate([halo, x[t0:t0 + TOK]], axis=0).reshape(NTILE, P, D))
        wmod_sl = np.ascontiguousarray(
            wmod[:, c * MODW:(c + 1) * MODW].reshape(NCH, P, MODW))
        veccols = np.zeros((P, 66), np.float32)
        veccols[:, 0:16] = cols_pack(decay)
        veccols[:, 16:32] = cols_pack(b1)
        veccols[:, 32:48] = cols_pack(b2)
        veccols[:, 48:64] = cols_pack(cvec)
        veccols[:, 64] = 0.0 if c == 0 else 1.0
        veccols[:, 65] = 1e-5
        in_maps.append({
            "xh": xh, "winT": winT, "woutT": woutT, "wffnT": wffnT,
            "wmod_sl": wmod_sl, "veccols": veccols,
            "identb": identb, "identf": identf,
            "onesrow": onesrow, "onescol": onescol,
        })

    if not _NC_CACHE:
        _NC_CACHE.append(build_program())
    nc = _NC_CACHE[0]
    res = run_bass_kernel_spmd(nc, in_maps, core_ids=list(range(NCORE)))
    out = np.concatenate(
        [np.asarray(r["out"]).reshape(TOK, D) for r in res.results], axis=0)
    return out.reshape(1, 8192, D).astype(np.float32)


# version 23 (final)
# speedup vs baseline: 1.0086x; 1.0069x over previous
import contextlib
import numpy as np
import ml_dtypes
import concourse.bass as bass
import concourse.mybir as mybir
from concourse.bass_utils import run_bass_kernel_spmd

F32 = mybir.dt.float32
BF16 = mybir.dt.bfloat16
AF = mybir.ActivationFunctionType
OP = mybir.AluOpType
AX = mybir.AxisListType

P = 128
D = 2048
NCH = 16
TOK = 1024
HAL = 128
TT = TOK + HAL
NTILE = TT // P
NCORE = 8
MODW = 12288 // NCORE
INV_D = 1.0 / D


class Ctx:

    def __init__(self):
        self.marks = {}
        self.mode = "count"
        self.cnt = {}
        self.eng = {}
        self.sem = {}

    def start(self, mode):
        self.mode = mode
        self.cnt = {k: 0 for k in ("c", "x0", "x1", "x2", "a", "e", "g", "o", "w0", "w1", "wo0", "wo1", "cc", "t", "v", "s")}

    def op(self, fn, inc=None, mark=None, waits=(), eng=None):
        selfser = inc is not None and inc[0] in ("v", "s")
        if self.mode == "emit":
            e = None
            if selfser:
                e = self.eng["ve"] if inc[0] == "v" else self.eng["sc"]
            elif waits:
                e = self.eng[eng]
            for sk, mk in waits:
                e.wait_ge(self.sem[sk], self.marks[mk])
            if selfser and self.cnt[inc[0]] > 0:
                e.wait_ge(self.sem[inc[0]], self.cnt[inc[0]])
            inst = fn() if fn is not None else None
            if inc is not None and inst is not None:
                inst.then_inc(self.sem[inc[0]], inc[1])
        if inc is not None:
            self.cnt[inc[0]] += inc[1]
        if mark is not None:
            if self.mode == "count":
                self.marks[mark] = self.cnt[inc[0]]
            else:
                assert self.marks[mark] == self.cnt[inc[0]], (
                    mark, self.marks[mark], self.cnt[inc[0]])

    def mark_here(self, name, sem_key):
        if self.mode == "count":
            self.marks[name] = self.cnt[sem_key]
        else:
            assert self.marks[name] == self.cnt[sem_key], name


def build_program():
    nc = bass.Bass("TRN2", target_bir_lowering=False)

    xh_d = nc.dram_tensor("xh", [NTILE, P, D], F32, kind="ExternalInput").ap()
    winT_d = nc.dram_tensor("winT", [NCH, P, D], F32, kind="ExternalInput").ap()
    woutT_d = nc.dram_tensor("woutT", [NCH, P, D], F32, kind="ExternalInput").ap()
    wffnT_d = nc.dram_tensor("wffnT", [NCH, P, D], F32, kind="ExternalInput").ap()
    wmod_d = nc.dram_tensor("wmod_sl", [NCH, P, MODW], F32, kind="ExternalInput").ap()
    cols_d = nc.dram_tensor("veccols", [P, 66], F32, kind="ExternalInput").ap()
    identb_d = nc.dram_tensor("identb", [P, P], BF16, kind="ExternalInput").ap()
    identf_d = nc.dram_tensor("identf", [P, P], F32, kind="ExternalInput").ap()
    onesrow_d = nc.dram_tensor("onesrow", [1, P], BF16, kind="ExternalInput").ap()
    onescol_d = nc.dram_tensor("onescol", [P, 1], BF16, kind="ExternalInput").ap()
    out_d = nc.dram_tensor("out", [TOK // P, P, D], F32, kind="ExternalOutput").ap()

    cc_a = nc.dram_tensor("cc_a", [1, MODW], F32, kind="Internal")
    scr_ga = nc.dram_tensor("scr_ga", [1, D], BF16, kind="Internal")
    scr_gab = nc.dram_tensor("scr_gab", [1, D], BF16, kind="Internal")
    cc_b = nc.dram_tensor("cc_b", [NCORE, MODW], F32, kind="Internal")
    cc_b96 = cc_b.ap().rearrange("r (c k) -> (r c) k", k=P)

    C = Ctx()

    with contextlib.ExitStack() as ST:
        DTSZ = {F32: 4, BF16: 2}
        _cur = [(nc.sbuf_base + 31) & ~31]

        def at(name, shape, dt, cursor):
            off = (cursor[0] + 31) & ~31
            nbytes = shape[1] * DTSZ[dt]
            cursor[0] = off + nbytes
            assert cursor[0] <= nc.sbuf_top, (name, cursor[0], nc.sbuf_top)
            return nc.alloc_sbuf_tensor_at(name, shape, dt, offset=off)

        def sb(name, shape, dt):
            return at(name, shape, dt, _cur)

        def psmk(name, shape, dt):
            return ST.enter_context(nc.psum_tensor(name, shape, dt))

        st_states = sb("st_states", [P, NCH * TOK], BF16)
        st_wout = sb("st_wout", [P, NCH * D], BF16)
        st_cols = sb("st_cols", [P, 352], F32)
        st_colsbf = sb("st_colsbf", [P, 16], BF16)
        st_gabrow = sb("st_gabrow", [1, D], BF16)
        st_onesrow = sb("st_onesrow", [1, P], BF16)
        st_onescol = sb("st_onescol", [P, 1], BF16)
        st_identb = sb("st_identb", [P, P], BF16)
        st_identf = sb("st_identf", [P, P], F32)
        st_lnmu = sb("st_lnmu", [P, NTILE], F32)
        st_lnr = sb("st_lnr", [P, NTILE], F32)
        st_lnb = sb("st_lnb", [P, NTILE], F32)
        st_ss = sb("st_ss", [P, 8], F32)

        CL = {}
        _b = 0
        for nm, w in [("decay", 16), ("b1", 16), ("b2", 16), ("cvec", 16),
                      ("mask", 1), ("eps", 1),
                      ("d", 16), ("omd", 16), ("omdm", 16),
                      ("onepca", 16), ("sa", 16), ("opcam", 16), ("sam", 16),
                      ("onepcb", 16), ("sbv", 16),
                      ("ga", 16), ("gb", 16), ("gbb2", 16), ("gba", 16),
                      ("t1", 16), ("t2", 16), ("siluc", 16)]:
            CL[nm] = _b
            _b += w
        assert _b <= 352

        def col(nm, k=0, w=1):
            return st_cols[:, CL[nm] + k: CL[nm] + k + w]

        psA0 = psmk("psA0", [P, 512], F32)
        psA1 = psmk("psA1", [P, 512], F32)
        psA2 = psmk("psA2", [P, 512], F32)
        psB0 = psmk("psB0", [P, 512], F32)
        psB1 = psmk("psB1", [P, 512], F32)
        psB2 = psmk("psB2", [P, 512], F32)
        psM = psmk("psM", [P, 512], F32)
        psTr = psmk("psTr", [P, 1024], BF16)

        _curA = [_cur[0]]
        _curB = [_cur[0]]

        def sbA(name, shape, dt):
            return at(name, shape, dt, _curA)

        def sbB(name, shape, dt):
            return at(name, shape, dt, _curB)

        sA_m1in = sbA("sA_m1in", [P, NCH * TT], BF16)
        sA_xst = sbA("sA_xst", [P, 2 * D], F32)
        sA_wmst = sbA("sA_wmst", [P, 2 * MODW], F32)
        sA_wist = sbA("sA_wist", [P, 2 * D], BF16)
        sA_scanin = sbA("sA_scanin", [P, TOK], F32)
        sA_scanh = sbA("sA_scanh", [P, HAL], F32)
        sA_hscr = sbA("sA_hscr", [P, HAL], F32)
        sA_ccrow = sbA("sA_ccrow", [1, MODW], F32)
        sA_bga = sbA("sA_bga", [P, D], BF16)
        sA_ext = sbA("sA_ext", [16, 6 * P], F32)
        sA_rows16 = sbA("sA_rows16", [16, 2 * P], BF16)
        sA_garow = sbA("sA_garow", [1, D], BF16)
        sA_n1 = sbA("sA_n1", [P, 2 * D], BF16)

        sB_x1 = sbB("sB_x1", [P, 8 * D], F32)
        sB_xst = sbB("sB_xst", [P, D], F32)
        sB_wblk = sbB("sB_wblk", [P, 2 * D], BF16)
        sB_sgn = sbB("sB_sgn", [P, 2 * D], BF16)
        sB_ffnsc = sbB("sB_ffnsc", [P, 1024], BF16)
        sB_sq = sbB("sB_sq", [P, D], BF16)
        sB_n2 = sbB("sB_n2", [P, 2 * D], BF16)
        sB_arow = sbB("sB_arow", [1, P], F32)

        for key in ("c", "x0", "x1", "x2", "a", "e", "g", "o",
                    "w0", "w1", "wo0", "wo1", "cc", "t", "v", "s"):
            C.sem[key] = ST.enter_context(nc.semaphore("sem_" + key))

        o = C.op

        def prog_sync(sync):
            C.eng["sync"] = sync
            o(lambda: sync.dma_start(out=st_cols[:, 0:66], in_=cols_d[:, :]),
              inc=("c", 16))
            o(lambda: sync.dma_start(out=st_identb[:], in_=identb_d[:, :]), inc=("c", 16))
            o(lambda: sync.dma_start(out=st_identf[:], in_=identf_d[:, :]), inc=("c", 16))
            o(lambda: sync.dma_start(out=st_onesrow[:], in_=onesrow_d[:, :]), inc=("c", 16))
            o(lambda: sync.dma_start(out=st_onescol[:], in_=onescol_d[:, :]),
              inc=("c", 16), mark="consts")
            for t in range(NTILE):
                waits = [("s", f"n1_{t - 2}")] if t >= 2 else ()
                o(lambda t=t: sync.dma_start(
                    out=sA_xst[:, (t % 2) * D:(t % 2) * D + D], in_=xh_d[t]),
                  inc=("x" + str(t % 2), 16), mark=f"x_{t}", waits=waits, eng="sync")
            o(lambda: sync.dma_start(out=cc_a.ap()[:, :], in_=sA_ccrow[0:1, :]),
              inc=("a", 16), mark="ccA", waits=[("v", "ccrow")], eng="sync")
            src3 = cc_b.ap().rearrange("r c -> (r c)").rearrange(
                "(j i k) -> i j k", j=6, i=16, k=P)
            dst3 = sA_ext[0:16, 0:6 * P].rearrange("p (j k) -> p j k", j=6)
            o(lambda: sync.dma_start(out=dst3, in_=src3),
              inc=("e", 16), mark="ext_all",
              waits=[("cc", "cc_done")], eng="sync")
            o(lambda: sync.dma_start(
                out=scr_ga.ap().rearrange("a (i p) -> i p", i=16),
                in_=sA_rows16[0:16, 0:P]),
              inc=("g", 16), mark="garow_d", waits=[("v", "rows16")], eng="sync")
            o(lambda: sync.dma_start(out=st_gabrow[0:1, :], in_=sA_rows16[0:16, P:2 * P]),
              inc=("g", 16), mark="rows_all")
            o(lambda: sync.dma_start(
                out=sA_bga[:], in_=scr_ga.ap()[0:1, :].broadcast_to([P, D])),
              inc=("g", 16), mark="bga_dma", waits=[("g", "rows_all")], eng="sync")
            o(lambda: sync.dma_start(
                out=scr_gab.ap().rearrange("a (i p) -> i p", i=16),
                in_=sA_rows16[0:16, P:2 * P]),
              inc=("g", 16), mark="gabrow_d", waits=[("g", "bga_dma")],
              eng="sync")
            o(lambda: sync.dma_start(
                out=sA_bga[:], in_=scr_gab.ap()[0:1, :].broadcast_to([P, D])),
              inc=("g", 16), mark="bgab_dma",
              waits=[("v", "fold_all"), ("g", "gabrow_d")], eng="sync")
            for t in range(8):
                waits = ([("v", "scan_all"), ("t", "mm1_all")] if t == 0
                         else [("v", f"ep1_{t - 1}")])
                o(lambda t=t: sync.dma_start(out=sB_xst[:], in_=xh_d[t + 1]),
                  inc=("x2", 16), mark=f"x2_{t}", waits=waits, eng="sync")
            out_r = out_d.rearrange("t p (m c) -> p m t c", c=P)
            x1_r = sB_x1[:, 0:8 * D].rearrange("p (t m c) -> p m t c", t=8, c=P)
            for m in range(NCH - 1):
                o(lambda m=m: sync.dma_start(
                    out=out_r[:, m:m + 1], in_=x1_r[:, m:m + 1]),
                  inc=("o", 16), mark=f"out_{m}",
                  waits=[("v", f"addv_{m}")], eng="sync")
            for t in range(8):
                o(lambda t=t: sync.dma_start(
                    out=out_r[:, NCH - 1:NCH, t:t + 1],
                    in_=x1_r[:, NCH - 1:NCH, t:t + 1]),
                  inc=("o", 16), waits=[("v", f"add15_{t}")], eng="sync")
            if C.mode == "emit":
                sync.wait_ge(C.sem["o"], (NCH - 1 + 8) * 16)

        def prog_gpsimd(gp):
            C.eng["gp"] = gp
            for c in range(NCH):
                waits = [("t", f"modmm_{c - 2}")] if c >= 2 else ()
                o(lambda c=c: gp.dma_start(
                    out=sA_wmst[:, (c % 2) * MODW:(c % 2) * MODW + MODW],
                    in_=wmod_d[c]),
                  inc=("w" + str(c % 2), 16), mark=f"wm_{c}", waits=waits, eng="gp")
            o(lambda: gp.collective_compute(
                "AllGather", OP.bypass,
                ins=[cc_a.ap()[:, :]], outs=[cc_b.ap()[:, :]],
                replica_groups=[list(range(NCORE))]),
              inc=("cc", 1), mark="cc_done", waits=[("a", "ccA")], eng="gp")
            for m in range(NCH):
                waits = [("t", f"mm1_{m - 2}")] if m >= 2 else ()
                o(lambda m=m: gp.dma_start(
                    out=sA_wist[:, (m % 2) * D:(m % 2) * D + D], in_=winT_d[m]),
                  inc=("w" + str(m % 2), 16), mark=f"wi_{m}", waits=waits, eng="gp")
                wo_waits = [("v", f"fold_{m - 2}")] if m >= 2 else ()
                o(lambda m=m: gp.dma_start(
                    out=st_wout[:, m * D:(m + 1) * D], in_=woutT_d[m]),
                  inc=("wo" + str(m % 2), 16), mark=f"wo_{m}", waits=wo_waits,
                  eng="gp")
            for m in range(NCH):
                if m <= 1:
                    waits = [("v", "scan_all")]
                else:
                    waits = [("t", f"alpha_{m - 2}")]
                o(lambda m=m: gp.dma_start(
                    out=sB_wblk[:, (m % 2) * D:(m % 2) * D + D], in_=wffnT_d[m]),
                  inc=("w" + str(m % 2), 16), mark=f"wf_{m}", waits=waits, eng="gp")

        def prog_scalar(sc):
            C.eng["sc"] = sc
            o(lambda: sc.activation(col("siluc", 0, 16), col("cvec", 0, 16), AF.Silu),
              inc=("s", 1), mark="siluc", waits=[("c", "consts")], eng="sc")
            o(lambda: sc.activation(col("d", 0, 16), col("decay", 0, 16), AF.Sigmoid),
              inc=("s", 1), mark="dsig")
            for t in range(NTILE):
                xs = sA_xst[:, (t % 2) * D:(t % 2) * D + D]
                n1 = sA_n1[:, (t % 2) * D:(t % 2) * D + D]
                waits = [("x" + str(t % 2), f"x_{t}")]
                if t >= 2:
                    waits.append(("t", f"tr1_{t - 2}"))
                o(lambda xs=xs, n1=n1: sc.activation(
                    n1, xs, AF.Square, accum_out=st_ss[:, 0:1]),
                  inc=("s", 1), mark=f"sq_{t}", waits=waits, eng="sc")
                o(lambda: sc.activation(st_ss[:, 3:4], st_ss[:, 1:2], AF.Sqrt,
                                        bias=col("eps")),
                  inc=("s", 1), mark=f"sd_{t}", waits=[("v", f"var_{t}")], eng="sc")
                o(lambda t=t, xs=xs, n1=n1: sc.activation(
                    n1, xs, AF.Identity,
                    bias=st_lnb[:, t:t + 1], scale=st_lnr[:, t:t + 1]),
                  inc=("s", 1), mark=f"n1_{t}", waits=[("v", f"rstd_{t}")], eng="sc")
            def s_trevac0(tt):
                for i in range(8):
                    waits = [("t", f"tr2r_{tt}_0")] if i == 0 else ()
                    o(lambda i=i, tt=tt: sc.activation(
                        st_states[:, i * TOK + tt * P:i * TOK + (tt + 1) * P],
                        psTr[:, i * P:(i + 1) * P], AF.Identity,
                        bias=col("sbv", i), scale=col("onepcb", i)),
                      inc=("s", 1),
                      mark=(f"ev2s_{tt}" if i == 7 else None),
                      waits=waits, eng="sc")

            for t in range(8):
                x1t = sB_x1[:, t * D:(t + 1) * D]
                if t > 1:
                    s_trevac0(t - 2)
                o(lambda x1t=x1t: sc.activation(
                    sB_sq[:], x1t, AF.Square, accum_out=st_ss[:, 0:1]),
                  inc=("s", 1), mark=f"sq2_{t}", waits=[("v", f"ep1_{t}")], eng="sc")
                o(lambda: sc.activation(st_ss[:, 3:4], st_ss[:, 2:3], AF.Sqrt,
                                        bias=col("eps")),
                  inc=("s", 1), mark=f"sd2_{t}", waits=[("v", f"var2_{t}")], eng="sc")
                waits = [("v", f"rstd2_{t}")]
                if t >= 2:
                    waits.append(("t", f"tr2_{t - 2}"))
                o(lambda t=t, x1t=x1t: sc.activation(
                    sB_n2[:, (t % 2) * D:(t % 2) * D + D], x1t, AF.Identity,
                    bias=st_ss[:, 5:6], scale=st_ss[:, 4:5]),
                  inc=("s", 1), mark=f"n2_{t}", waits=waits, eng="sc")
            s_trevac0(6)
            s_trevac0(7)
            def s_sgnabs(m):
                wb = sB_wblk[:, (m % 2) * D:(m % 2) * D + D]
                sg = sB_sgn[:, (m % 2) * D:(m % 2) * D + D]
                waits = [("w" + str(m % 2), f"wf_{m}")]
                if m >= 2:
                    waits.append(("t", f"mm3_{m - 2}"))
                o(lambda wb=wb, sg=sg: sc.sign(sg, wb),
                  inc=("s", 1), mark=f"sgn_{m}", waits=waits, eng="sc")
                o(lambda wb=wb: sc.activation(wb, wb, AF.Abs),
                  inc=("s", 1), mark=f"abs_{m}")

            s_sgnabs(0)
            s_sgnabs(1)
            for m in range(1, NCH):
                if m >= 2:
                    s_sgnabs(m)
                if m > 0:
                    mp = m - 1
                    fbp = [psA0, psA1] if mp % 2 == 0 else [psA2, psB0]
                    for j in range(2):
                        waits = ([("t", f"mm3_{mp}"), ("v", f"gba_{mp}")]
                                 if j == 0 else ())
                        o(lambda mp=mp, j=j, fbp=fbp: sc.activation(
                            sB_ffnsc[:, j * 512:(j + 1) * 512], fbp[j][:, :],
                            AF.Identity, bias=col("gbb2", mp),
                            scale=col("gba", mp)),
                          inc=("s", 1), mark=(f"fev_{mp}" if j == 1 else None),
                          waits=waits, eng="sc")
            mp = NCH - 1
            fbp = [psA2, psB0]
            for j in range(2):
                waits = ([("t", f"mm3_{mp}"), ("v", f"gba_{mp}")] if j == 0 else ())
                o(lambda mp=mp, j=j, fbp=fbp: sc.activation(
                    sB_ffnsc[:, j * 512:(j + 1) * 512], fbp[j][:, :],
                    AF.Identity, bias=col("gbb2", mp), scale=col("gba", mp)),
                  inc=("s", 1), mark=(f"fev_{mp}" if j == 1 else None),
                  waits=waits, eng="sc")

        def prog_tensor(te):
            C.eng["te"] = te

            def transpose_tile_n1(t):
                n1 = sA_n1[:, (t % 2) * D:(t % 2) * D + D]
                for r in range(2):
                    for i in range(8):
                        k = r * 8 + i
                        waits = []
                        if i == 0:
                            if r == 0:
                                waits.append(("s", f"n1_{t}"))
                            if t == 0 and r == 0:
                                waits.append(("c", "consts"))
                            if t > 0 or r > 0:
                                pt, pr = (t, 0) if r == 1 else (t - 1, 1)
                                waits.append(("v", f"ev1_{pt}_{pr}"))
                        o(lambda k=k, n1=n1: te.transpose(
                            psTr[:, (k % 8) * P:(k % 8) * P + P],
                            n1[:, k * P:(k + 1) * P], st_identb[:]),
                          inc=("t", 1),
                          mark=(f"tr1r_{t}_{r}" if i == 7 else None),
                          waits=waits, eng="te")
                C.mark_here(f"tr1_{t}", "t")

            def emit_mod_chunk(c):
                wst = sA_wmst[:, (c % 2) * MODW:(c % 2) * MODW + MODW]
                banks = [psB0, psB1, psB2]
                for j in range(3):
                    o(lambda c=c, j=j, wst=wst, bk=banks[j]: te.matmul(
                        bk[0:1, :], col("siluc", c),
                        wst[:, j * 512:(j + 1) * 512],
                        start=(c == 0), stop=(c == NCH - 1)),
                      inc=("t", 1), mark=(f"modmm_{c}" if j == 2 else None),
                      waits=([("w" + str(c % 2), f"wm_{c}"), ("s", "siluc")] if j == 0 else ()),
                      eng="te")

            order = [("mod", 0), ("mod", 1), ("mod", 2), ("mod", 3)]
            for t in range(NTILE):
                order.append(("tile", t))
                if 4 + 2 * t < NCH:
                    order.append(("mod", 4 + 2 * t))
                if 5 + 2 * t < NCH:
                    order.append(("mod", 5 + 2 * t))
            for kind, idx in order:
                if kind == "mod":
                    emit_mod_chunk(idx)
                else:
                    transpose_tile_n1(idx)

            for j in range(6):
                o(lambda j=j: te.transpose(
                    psM[:, j * 16:(j + 1) * 16],
                    sA_ext[0:16, j * P:(j + 1) * P], st_identf[0:16, 0:16]),
                  inc=("t", 1), mark=f"ccT_{j}",
                  waits=[("e", "ext_all")], eng="te")

            o(lambda: te.transpose(psM[0:16, 224:224 + P], col("ga", 0, 16),
                                   st_identf[:]),
              inc=("t", 1), mark="gaT",
              waits=[("v", "gacols"), ("v", "ccev")], eng="te")
            o(lambda: te.transpose(psM[0:16, 352:352 + P], col("t2", 0, 16),
                                   st_identf[:]),
              inc=("t", 1), mark="gabT", waits=[("v", "gabcols")], eng="te")

            for m in range(NCH):
                wst = sA_wist[:, (m % 2) * D:(m % 2) * D + D]
                pb = [psA0, psA1, psA2] if m % 2 == 0 else [psB0, psB1, psB2]
                segs = [(0, 512, 0), (512, 512, 1), (1024, HAL, 2)]
                for (off, ln, b) in segs:
                    for k in range(NCH):
                        waits = []
                        if k == 0 and off == 0:
                            waits = [("w" + str(m % 2), f"wi_{m}"), ("v", "affine_all")]
                            if m >= 2:
                                waits.append(("v", f"scanprep_{m - 2}"))
                            if m == 1:
                                waits.append(("v", "ccrow"))
                            if m == 0:
                                waits.append(("v", f"ev1_{NTILE - 1}_1"))
                        o(lambda k=k, off=off, ln=ln, b=b, wst=wst, pb=pb: te.matmul(
                            pb[b][:, 0:ln], wst[:, k * P:(k + 1) * P],
                            sA_m1in[:, k * TT + off:k * TT + off + ln],
                            start=(k == 0), stop=(k == NCH - 1)),
                          inc=("t", 1),
                          mark=(f"mm1_{m}" if (k == NCH - 1 and b == 2) else None),
                          waits=waits, eng="te")
            C.mark_here("mm1_all", "t")

            def transpose_round_n2(t, r):
                for i in range(8):
                    k = r * 8 + i
                    waits = []
                    if i == 0:
                        if r == 0:
                            waits.append(("s", f"n2_{t}"))
                            if t > 0:
                                waits.append(("v", f"ev2_{t - 1}_1"))
                            else:
                                waits.append(("v", f"ev1_{NTILE - 1}_1"))
                        else:
                            waits.append(("s", f"ev2s_{t}"))
                    o(lambda k=k, t=t: te.transpose(
                        psTr[:, (k % 8) * P:(k % 8) * P + P],
                        sB_n2[:, (t % 2) * D + k * P:(t % 2) * D + (k + 1) * P],
                        st_identb[:]),
                      inc=("t", 1),
                      mark=(f"tr2r_{t}_{r}" if i == 7 else None),
                      waits=waits, eng="te")
                if r == 1:
                    C.mark_here(f"tr2_{t}", "t")

            ybanks = [psA0, psA1, psA2, psB0]
            for t in range(8):
                for half in range(2):
                    for jj in range(2):
                        j = half * 2 + jj
                        for k in range(NCH):
                            waits = []
                            if k == 0 and jj == 0:
                                if t == 0 and half == 0:
                                    waits = [("v", "scan_all"), ("v", "fold_all"),
                                             ("v", "scanprep_15")]
                                elif t > 0:
                                    waits = [("v", f"ep1h_{t - 1}_{half}")]
                            o(lambda t=t, k=k, j=j: te.matmul(
                                ybanks[j][:, :],
                                st_states[:, k * TOK + t * P:k * TOK + (t + 1) * P],
                                st_wout[:, k * D + j * 512:k * D + (j + 1) * 512],
                                start=(k == 0), stop=(k == NCH - 1)),
                              inc=("t", 1),
                              mark=(f"mm2_{t}_{half}"
                                    if (jj == 1 and k == NCH - 1) else None),
                              waits=waits, eng="te")
                    if t > 1:
                        transpose_round_n2(t - 2, half)
            transpose_round_n2(6, 0)
            transpose_round_n2(6, 1)
            transpose_round_n2(7, 0)
            transpose_round_n2(7, 1)

            def emit_trf(mm):
                for i in range(8):
                    waits = []
                    if i == 0:
                        waits = [("s", f"fev_{mm}")]
                        waits.append(("v", f"addv_{mm - 1}") if mm > 0
                                     else ("v", "ev2_7_1"))
                    o(lambda mm=mm, i=i: te.transpose(
                        psTr[:, i * P:(i + 1) * P],
                        sB_ffnsc[:, i * P:(i + 1) * P], st_identb[:]),
                      inc=("t", 1), mark=(f"trf_{mm}" if i == 7 else None),
                      waits=waits, eng="te")

            for m in range(NCH):
                abank = psB1 if m % 2 == 0 else psB2
                fb = [psA0, psA1] if m % 2 == 0 else [psA2, psB0]
                for k in range(NCH):
                    waits = []
                    if k == 0:
                        waits = [("s", f"abs_{m}")]
                        if m >= 2:
                            waits.append(("v", f"aev1_{m - 2}"))
                    o(lambda m=m, k=k, ab=abank: te.matmul(
                        ab[0:1, 0:P], st_onescol[:, 0:1],
                        sB_wblk[:, (m % 2) * D + k * P:(m % 2) * D + (k + 1) * P],
                        start=(k == 0), stop=(k == NCH - 1)),
                      inc=("t", 1), mark=(f"alpha_{m}" if k == NCH - 1 else None),
                      waits=waits, eng="te")
                cnt = 0
                for j in range(2):
                    for k in range(NCH):
                        waits = []
                        if cnt == 0:
                            waits = [("s", f"sgn_{m}")]
                            if m == 0:
                                waits.append(("v", "ev2_3_1"))
                                waits.append(("v", "ep1h_7_0"))
                            if m >= 2:
                                waits.append(("s", f"fev_{m - 2}"))
                        if cnt == NCH and m == 0:
                            waits.append(("v", "tr2ev_all"))
                        o(lambda m=m, k=k, j=j, fb=fb: te.matmul(
                            fb[j][:, :],
                            sB_sgn[:, (m % 2) * D + k * P:(m % 2) * D + (k + 1) * P],
                            st_states[:, k * TOK + j * 512:k * TOK + (j + 1) * 512],
                            start=(k == 0), stop=(k == NCH - 1)),
                          inc=("t", 1),
                          mark=(f"mm3_{m}" if (k == NCH - 1 and j == 1) else None),
                          waits=waits, eng="te")
                        cnt += 1
                        if cnt == 8:
                            o(lambda m=m: te.transpose(
                                psM[:, 256 + (m % 2):256 + (m % 2) + 1],
                                sB_arow[0:1, :], st_identf[0:1, 0:1]),
                              inc=("t", 1), mark=f"alT_{m}",
                              waits=[("v", f"aev1_{m}")], eng="te")
                    if j == 0 and m > 0:
                        emit_trf(m - 1)
                    if j == 1 and m == NCH - 1:
                        emit_trf(NCH - 1)

        def prog_vector(ve):
            C.eng["ve"] = ve

            def ln_stats(xs_ap, mu_ap, r_ap, b_ap, e2_ap, mu2_ap,
                         varmark, rstdmark, sqm, sdm, waits0):
                o(lambda: ve.reduce_sum(mu_ap, xs_ap, axis=AX.X),
                  inc=("v", 1), waits=waits0, eng="ve")
                o(lambda: ve.tensor_scalar(mu_ap, mu_ap, INV_D, None, op0=OP.mult),
                  inc=("v", 1))
                o(lambda: ve.tensor_scalar(e2_ap, st_ss[:, 0:1], INV_D, None,
                                           op0=OP.mult),
                  inc=("v", 1), waits=[("s", sqm)], eng="ve")
                o(lambda: ve.tensor_mul(mu2_ap, mu_ap, mu_ap), inc=("v", 1))
                o(lambda: ve.tensor_sub(e2_ap, e2_ap, mu2_ap),
                  inc=("v", 1), mark=varmark)
                o(lambda: ve.reciprocal(r_ap, st_ss[:, 3:4]),
                  inc=("v", 1), waits=[("s", sdm)], eng="ve")
                o(lambda: ve.tensor_scalar(b_ap, mu_ap, r_ap, -1.0,
                                           op0=OP.mult, op1=OP.mult),
                  inc=("v", 1), mark=rstdmark)

            def evac_tile_n1(t):
                for r in range(2):
                    for i in range(8):
                        k = r * 8 + i
                        waits = [("t", f"tr1r_{t}_{r}")] if i == 0 else ()
                        o(lambda k=k, t=t: ve.tensor_copy(
                            sA_m1in[:, k * TT + t * P:k * TT + (t + 1) * P],
                            psTr[:, (k % 8) * P:(k % 8) * P + P]),
                          inc=("v", 1),
                          mark=(f"ev1_{t}_{r}" if i == 7 else None),
                          waits=waits, eng="ve")

            for t in range(NTILE):
                xs = sA_xst[:, (t % 2) * D:(t % 2) * D + D]
                ln_stats(xs, st_lnmu[:, t:t + 1], st_lnr[:, t:t + 1],
                         st_lnb[:, t:t + 1], st_ss[:, 1:2], st_ss[:, 2:3],
                         f"var_{t}", f"rstd_{t}", f"sq_{t}", f"sd_{t}",
                         [("x" + str(t % 2), f"x_{t}")])
                if t > 0:
                    evac_tile_n1(t - 1)
                if t == 6:
                    for j, bk in enumerate([psB0, psB1, psB2]):
                        o(lambda j=j, bk=bk: ve.tensor_copy(
                            sA_ccrow[0:1, j * 512:(j + 1) * 512], bk[0:1, :]),
                          inc=("v", 1), mark=("ccrow" if j == 2 else None),
                          waits=[("t", "modmm_15")] if j == 0 else (), eng="ve")
            evac_tile_n1(NTILE - 1)

            o(lambda: ve.tensor_copy(col("sa", 0, 16), psM[:, 0:16]),
              inc=("v", 1), waits=[("t", "ccT_5")], eng="ve")
            o(lambda: ve.tensor_scalar(col("onepca", 0, 16), psM[:, 16:32], 1.0,
                                       None, op0=OP.add), inc=("v", 1))
            def vclamp(psl, outnm, markn):
                o(lambda: ve.tensor_scalar(col("t1", 0, 16), psl, -1.0, None,
                                           op0=OP.mult), inc=("v", 1))
                o(lambda: ve.tensor_max(col("t1", 0, 16), col("t1", 0, 16), psl),
                  inc=("v", 1))
                o(lambda: ve.tensor_scalar(col("t2", 0, 16), col("t1", 0, 16),
                                           1e-30, None, op0=OP.max), inc=("v", 1))
                o(lambda: ve.reciprocal(col("t2", 0, 16), col("t2", 0, 16)),
                  inc=("v", 1))
                o(lambda: ve.tensor_mul(col("t2", 0, 16), col("t2", 0, 16), psl),
                  inc=("v", 1))
                o(lambda: ve.tensor_scalar(col("t1", 0, 16), col("t1", 0, 16),
                                           0.25, None, op0=OP.max), inc=("v", 1))
                o(lambda: ve.tensor_mul(col(outnm, 0, 16), col("t1", 0, 16),
                                        col("t2", 0, 16)),
                  inc=("v", 1), mark=markn)

            vclamp(psM[:, 32:48], "ga", "gacols")
            o(lambda: ve.tensor_copy(col("sbv", 0, 16), psM[:, 48:64]), inc=("v", 1))
            o(lambda: ve.tensor_scalar(col("onepcb", 0, 16), psM[:, 64:80], 1.0,
                                       None, op0=OP.add), inc=("v", 1))
            vclamp(psM[:, 80:96], "gb", "ccev")
            o(lambda: ve.tensor_mul(col("gbb2", 0, 16), col("gb", 0, 16),
                                    col("b2", 0, 16)), inc=("v", 1))
            o(lambda: ve.tensor_mul(col("t2", 0, 16), col("ga", 0, 16),
                                    col("b1", 0, 16)), inc=("v", 1),
              mark="gabcols")
            o(lambda: ve.tensor_scalar(col("omd", 0, 16), col("d", 0, 16), -1.0,
                                       1.0, op0=OP.mult, op1=OP.add),
              inc=("v", 1), waits=[("s", "dsig")], eng="ve")
            o(lambda: ve.tensor_scalar(col("omdm", 0, 16), col("omd", 0, 16),
                                       col("mask"), None, op0=OP.mult),
              inc=("v", 1))
            o(lambda: ve.tensor_scalar(col("opcam", 0, 16), col("onepca", 0, 16),
                                       col("mask"), None, op0=OP.mult), inc=("v", 1))
            o(lambda: ve.tensor_scalar(col("sam", 0, 16), col("sa", 0, 16),
                                       col("mask"), None, op0=OP.mult),
              inc=("v", 1), mark="affscal")

            o(lambda: ve.tensor_copy(sA_rows16[0:16, 0:P], psM[0:16, 224:224 + P]),
              inc=("v", 1), waits=[("t", "gabT")], eng="ve")
            o(lambda: ve.tensor_copy(sA_rows16[0:16, P:2 * P],
                                     psM[0:16, 352:352 + P]),
              inc=("v", 1), mark="rows16")

            for k in range(NCH):
                waits = []
                if k == 0:
                    waits = [("v", f"ev1_{NTILE - 1}_1"), ("v", "affscal")]
                o(lambda k=k: ve.tensor_scalar(
                    sA_m1in[:, k * TT:k * TT + HAL],
                    sA_m1in[:, k * TT:k * TT + HAL],
                    col("opcam", k), col("sam", k), op0=OP.mult, op1=OP.add),
                  inc=("v", 1), waits=waits, eng="ve")
                o(lambda k=k: ve.tensor_scalar(
                    sA_m1in[:, k * TT + HAL:(k + 1) * TT],
                    sA_m1in[:, k * TT + HAL:(k + 1) * TT],
                    col("onepca", k), col("sa", k), op0=OP.mult, op1=OP.add),
                  inc=("v", 1),
                  mark=("affine_all" if k == NCH - 1 else None))

            for m in range(NCH):
                pb = [psA0, psA1, psA2] if m % 2 == 0 else [psB0, psB1, psB2]
                waits = [("wo" + str(m % 2), f"wo_{m}")]
                if m == 0:
                    waits.append(("g", "bga_dma"))
                o(lambda m=m: ve.tensor_mul(
                    st_wout[:, m * D:(m + 1) * D], st_wout[:, m * D:(m + 1) * D],
                    sA_bga[:]),
                  inc=("v", 1), mark=(f"fold_{m}"), waits=waits, eng="ve")
                if m == NCH - 1:
                    C.mark_here("fold_all", "v")
                o(lambda m=m, pb=pb: ve.tensor_scalar(
                    sA_scanh[:], pb[0][:, 0:HAL], col("omdm", m), None,
                    op0=OP.mult),
                  inc=("v", 1), waits=[("t", f"mm1_{m}")], eng="ve")
                o(lambda m=m, pb=pb: ve.tensor_scalar(
                    sA_scanin[:, 0:384], pb[0][:, HAL:512], col("omd", m), None,
                    op0=OP.mult), inc=("v", 1))
                o(lambda m=m, pb=pb: ve.tensor_scalar(
                    sA_scanin[:, 384:896], pb[1][:, :], col("omd", m), None,
                    op0=OP.mult), inc=("v", 1))
                o(lambda m=m, pb=pb: ve.tensor_scalar(
                    sA_scanin[:, 896:1024], pb[2][:, 0:HAL], col("omd", m), None,
                    op0=OP.mult),
                  inc=("v", 1), mark=f"scanprep_{m}")
                o(lambda m=m: ve.tensor_tensor_scan(
                    sA_hscr[:], col("d", m).broadcast_to([P, HAL]), sA_scanh[:],
                    0.0, op0=OP.mult, op1=OP.add), inc=("v", 1))
                o(lambda m=m: ve.tensor_tensor_scan(
                    st_states[:, m * TOK:(m + 1) * TOK],
                    col("d", m).broadcast_to([P, TOK]), sA_scanin[:],
                    sA_hscr[:, HAL - 1:HAL], op0=OP.mult, op1=OP.add),
                  inc=("v", 1), mark=(f"scan_all" if m == NCH - 1 else None))

            def trevac_tile(tt):
                r = 1
                for i in range(8):
                    k = r * 8 + i
                    waits = [("t", f"tr2r_{tt}_{r}")] if i == 0 else ()
                    o(lambda k=k, tt=tt: ve.tensor_scalar(
                        st_states[:, k * TOK + tt * P:k * TOK + (tt + 1) * P],
                        psTr[:, (k % 8) * P:(k % 8) * P + P],
                        col("onepcb", k), col("sbv", k),
                        op0=OP.mult, op1=OP.add),
                      inc=("v", 1),
                      mark=(f"ev2_{tt}_{r}" if i == 7 else None),
                      waits=waits, eng="ve")

            for t in range(8):
                ybanks = [psA0, psA1, psA2, psB0]
                xg_waits = [("x2", f"x2_{t}")]
                if t == 0:
                    xg_waits.append(("g", "bgab_dma"))
                o(lambda t=t: ve.tensor_add(sB_xst[:], sB_xst[:], sA_bga[:]),
                  inc=("v", 1), waits=xg_waits, eng="ve")
                for half in range(2):
                    for jj in range(2):
                        j = half * 2 + jj
                        waits = []
                        if jj == 0:
                            waits = [("t", f"mm2_{t}_{half}")]
                        o(lambda t=t, j=j, yb=ybanks[j]: ve.tensor_add(
                            sB_x1[:, t * D + j * 512:t * D + (j + 1) * 512],
                            yb[:, :], sB_xst[:, j * 512:(j + 1) * 512]),
                          inc=("v", 1),
                          mark=(f"ep1h_{t}_{half}" if jj == 1 else None),
                          waits=waits, eng="ve")
                C.mark_here(f"ep1_{t}", "v")
                if t > 1:
                    trevac_tile(t - 2)
                x1t = sB_x1[:, t * D:(t + 1) * D]
                ln_stats(x1t, st_ss[:, 1:2], st_ss[:, 4:5], st_ss[:, 5:6],
                         st_ss[:, 2:3], st_ss[:, 3:4],
                         f"var2_{t}", f"rstd2_{t}", f"sq2_{t}", f"sd2_{t}", ())
            trevac_tile(6)
            trevac_tile(7)
            C.mark_here("tr2ev_all", "v")

            def emit_adds(mm):
                for t in range(8):
                    waits = [("t", f"trf_{mm}")] if t == 0 else ()
                    o(lambda mm=mm, t=t: ve.tensor_add(
                        sB_x1[:, t * D + mm * P:t * D + (mm + 1) * P],
                        sB_x1[:, t * D + mm * P:t * D + (mm + 1) * P],
                        psTr[:, t * P:(t + 1) * P]),
                      inc=("v", 1),
                      mark=(f"add15_{t}" if mm == NCH - 1 else
                            (f"addv_{mm}" if t == 7 else None)),
                      waits=waits, eng="ve")

            for m in range(NCH):
                abank = psB1 if m % 2 == 0 else psB2
                fb = [psA0, psA1] if m % 2 == 0 else [psA2, psB0]
                o(lambda ab=abank: ve.tensor_scalar(
                    sB_arow[0:1, :], ab[0:1, 0:P], INV_D, None, op0=OP.mult),
                  inc=("v", 1), mark=f"aev1_{m}",
                  waits=[("t", f"alpha_{m}")], eng="ve")
                o(lambda m=m: ve.tensor_scalar(
                    col("gba", m), psM[:, 256 + (m % 2):256 + (m % 2) + 1],
                    col("gb", m), None, op0=OP.mult),
                  inc=("v", 1), mark=f"gba_{m}",
                  waits=[("t", f"alT_{m}")], eng="ve")
                if m > 0:
                    emit_adds(m - 1)
            emit_adds(NCH - 1)

        progs = [prog_sync, prog_gpsimd, prog_scalar, prog_tensor, prog_vector]
        C.start("count")
        for f in progs:
            f(None)
        C.start("emit")
        block = ST.enter_context(nc.Block())
        block.sync(prog_sync)
        block.gpsimd(prog_gpsimd)
        block.scalar(prog_scalar)
        block.tensor(prog_tensor)
        block.vector(prog_vector)

    return nc


_NC_CACHE = []


def kernel(L0_attn_x, L0_attn_W_in, L0_attn_W_out, L0_attn_decay, L0_attn_bias,
           L0_ffn_weight, L0_ffn_bias, L0_c, L0_wmod):
    x = np.ascontiguousarray(np.asarray(L0_attn_x, dtype=np.float32).reshape(8192, D))
    W_in = np.asarray(L0_attn_W_in, dtype=np.float32)
    W_out = np.asarray(L0_attn_W_out, dtype=np.float32)
    decay = np.asarray(L0_attn_decay, dtype=np.float32)
    b1 = np.asarray(L0_attn_bias, dtype=np.float32)
    wffn = np.asarray(L0_ffn_weight, dtype=np.float32)
    b2 = np.asarray(L0_ffn_bias, dtype=np.float32)
    cvec = np.asarray(L0_c, dtype=np.float32).reshape(D)
    wmod = np.asarray(L0_wmod, dtype=np.float32)

    def blocks_swizzle(A):
        return np.ascontiguousarray(
            A.reshape(NCH, P, NCH, P).transpose(2, 1, 0, 3).reshape(NCH, P, D))

    winT = blocks_swizzle(W_in.T)
    wffnT = blocks_swizzle(wffn.T)
    woutT = np.ascontiguousarray(W_out.T.reshape(NCH, P, D))

    def cols_pack(v):
        return np.ascontiguousarray(v.reshape(NCH, P).T)

    identb = np.eye(P, dtype=ml_dtypes.bfloat16)
    identf = np.eye(P, dtype=np.float32)
    onesrow = np.ones((1, P), dtype=ml_dtypes.bfloat16)
    onescol = np.ones((P, 1), dtype=ml_dtypes.bfloat16)

    in_maps = []
    for c in range(NCORE):
        t0 = c * TOK
        halo = (np.zeros((HAL, D), np.float32) if c == 0
                else x[t0 - HAL:t0])
        xh = np.ascontiguousarray(
            np.concatenate([halo, x[t0:t0 + TOK]], axis=0).reshape(NTILE, P, D))
        wmod_sl = np.ascontiguousarray(
            wmod[:, c * MODW:(c + 1) * MODW].reshape(NCH, P, MODW))
        veccols = np.zeros((P, 66), np.float32)
        veccols[:, 0:16] = cols_pack(decay)
        veccols[:, 16:32] = cols_pack(b1)
        veccols[:, 32:48] = cols_pack(b2)
        veccols[:, 48:64] = cols_pack(cvec)
        veccols[:, 64] = 0.0 if c == 0 else 1.0
        veccols[:, 65] = 1e-5
        in_maps.append({
            "xh": xh, "winT": winT, "woutT": woutT, "wffnT": wffnT,
            "wmod_sl": wmod_sl, "veccols": veccols,
            "identb": identb, "identf": identf,
            "onesrow": onesrow, "onescol": onescol,
        })

    if not _NC_CACHE:
        _NC_CACHE.append(build_program())
    nc = _NC_CACHE[0]
    res = run_bass_kernel_spmd(nc, in_maps, core_ids=list(range(NCORE)))
    out = np.concatenate(
        [np.asarray(r["out"]).reshape(TOK, D) for r in res.results], axis=0)
    return out.reshape(1, 8192, D).astype(np.float32)
